# version 09
# baseline (speedup 1.0000x reference)
import contextlib
import os
import numpy as np
import ml_dtypes
import concourse.bass as bass
import concourse.mybir as mybir
from concourse.bass_utils import run_bass_kernel_spmd

F32 = mybir.dt.float32
BF16 = mybir.dt.bfloat16
AF = mybir.ActivationFunctionType
ALU = mybir.AluOpType
AXX = mybir.AxisListType.X

D = 1024
NST = 16
I = 2816
GT = 8
NIB = I // 128
SECW = [512, 512, 512, 512, 512, 256]
SECO = [0, 512, 1024, 1536, 2048, 2560]
LN_EPS = 1e-5
M2 = [0.0625, 0.5625, 1.5625, 3.0625, 6.25, 12.25, 25.0]
M1 = [0.25, 0.75, 1.25, 1.75, 2.5, 3.5, 5.0]

PF_COLS = 24192
PB_COLS = 53504


class Plan:
    def __init__(self):
        self.p = {e: [] for e in ("sync", "vector", "scalar", "gpsimd", "tensor")}
        self.tot = {}

    def inc(self, sem, k):
        key = id(sem)
        self.tot[key] = self.tot.get(key, 0) + k
        return self.tot[key]

    def val(self, sem):
        return self.tot.get(id(sem), 0)


def build_program(L=8192, T=512):
    NCH = L // T
    TT = T // 128
    NTIL = L // 128
    NPAIR = NTIL // 2

    nc = bass.Bass("TRN2", target_bir_lowering=False)

    def din(name, shape, dtype=F32):
        return nc.dram_tensor(name, shape, dtype, kind="ExternalInput").ap()

    x_hbm = din("x_b", [L, D])
    w_inT = din("w_inT", [D, 2 * D])
    w_dtT = din("w_dtT", [D, D])
    w_outT = din("w_outT", [D, D])
    w_bcT = din("w_bcT", [D, 32])
    w_gate = din("w_gate", [I, D])
    w_up = din("w_up", [I, D])
    w_down = din("w_down", [D, I])
    acols_in = din("a_cols_log", [128, 128])
    binc_in = din("b_in_cols", [128, 16])
    bdtc_in = din("b_dt_cols", [128, 8])
    bbc_in = din("bbc_col", [32, 1])
    dcol_in = din("d_cols", [128, 8])
    boutr_in = din("b_out_rep", [128, D], BF16)
    gamr_in = din("gamma_rep", [128, D], BF16)
    betr_in = din("beta_rep", [128, D], BF16)
    ident_in = din("ident_bf", [128, 128], BF16)
    consts_in = din("consts", [128, 8])

    out_hbm = nc.dram_tensor("out_b", [L, D], F32, kind="ExternalOutput").ap()

    def scr(name, shape, dtype):
        return nc.dram_tensor(name, shape, dtype, kind="Internal").ap()

    s_winT = scr("s_winT", [D, 2 * D], BF16)
    s_wdtT = scr("s_wdtT", [D, D], BF16)
    s_woutT = scr("s_woutT", [D, D], BF16)
    s_wbcT = scr("s_wbcT", [D, 32], BF16)
    s_wgT = scr("s_wgT", [D, I], BF16)
    s_wuT = scr("s_wuT", [D, I], BF16)
    s_wdnT = scr("s_wdnT", [I, D], BF16)
    s_xn = scr("s_xn", [L, D], F32)
    s_sg = scr("s_sg", [L, D], BF16)
    s_bc = scr("s_bc", [2, 32, T], BF16)

    E = Plan()
    es = contextlib.ExitStack()
    with es:
        PF = es.enter_context(nc.sbuf_tensor("PF", [128, PF_COLS], F32))
        PB = es.enter_context(nc.sbuf_tensor("PB", [128, PB_COLS], BF16))

        class Alloc:
            def __init__(self, pool, limit):
                self.pool, self.limit, self.off = pool, limit, 0

            def take(self, n):
                a = self.off
                self.off += n
                assert self.off <= self.limit, f"pool overflow {self.off}>{self.limit}"
                return self.pool[:, a:a + n]

        af = Alloc(PF, PF_COLS)
        ab = Alloc(PB, PB_COLS)

        consts = af.take(8)
        acols = af.take(128)
        binc = af.take(16)
        binh = af.take(16)
        bdtc = af.take(8)
        dcol = af.take(8)
        stats = af.take(64)
        bbc_f = af.take(32)
        ident = ab.take(128)
        boutr = ab.take(D)

        f_base = af.off
        b_base = ab.off

        xbuf = af.take(TT * D)
        pre = xbuf
        xn = af.take(TT * D)
        xss = af.take(GT * T)
        zs = af.take(GT * T)
        dts = af.take(GT * T)
        dA = af.take(4 * T)
        carry = af.take(2 * 128)
        ln_scr = af.take(2 * T)
        p1_f_end = af.off

        xnb = ab.take(TT * D)
        xnT = ab.take(GT * T)
        y2b = ab.take(GT * T)
        brep = ab.take(NST * T)
        crep = ab.take(NST * T)
        uhalf = ab.take(8 * T)
        hbuf = ab.take(8 * T)
        dtxb = ab.take(T)
        winr = ab.take(2 * GT * 128)
        wdtr = ab.take(2 * GT * 128)
        wout_r = ab.take(GT * D)
        wbc_r = ab.take(GT * 32)
        gamr = ab.take(D)
        betr = ab.take(D)
        p1_b_end = ab.off
        bcbf = hbuf[:, 0:T]

        af.off = f_base
        ab.off = b_base
        xn2 = af.take(4 * D)
        x2 = af.take(2 * D)
        qpool = af.take(2 * I)
        qstat = af.take(768)
        hstage = af.take(2 * I)
        uprow = af.take(2 * I)
        gsc = hstage[:, 0:2048]
        obuf = hstage[:, I:I + D]
        assert af.off <= p1_f_end, af.off

        wg_r = ab.take(GT * I)
        wur = ab.take(2 * GT * 512)
        dnr = ab.take(2 * D)
        xnq = ab.take(2 * D)
        xnqT = ab.take(2 * GT * 128)
        hq = ab.take(2 * I)
        hqT = ab.take(2 * NIB * 128)
        s2 = ab.take(4 * D)
        assert ab.off <= max(p1_b_end, ab.off)
        af.off = max(af.off, p1_f_end)
        ab.off = max(ab.off, p1_b_end)

        wstage = [hstage[:, 0:I], hstage[:, I:2 * I]]
        wqstage = [wg_r[:, 0:I], wg_r[:, I:2 * I]]
        wq_ev = wg_r[:, 2 * I:4 * I]

        PSF = es.enter_context(nc.psum_tensor("PSF", [128, 3072], F32))
        PSB = es.enter_context(nc.psum_tensor("PSB", [128, 2048], BF16))
        p_dn = [PSF[:, k * 512:(k + 1) * 512] for k in range(4)]
        pgu = [PSF[:, 2048:2560], PSF[:, 2560:3072]]
        p_mm = [pgu[0], pgu[1]]
        ptr = [PSB[:, 0:128], PSB[:, 1024:1152]]

        s_ld = es.enter_context(nc.semaphore("s_ld"))
        s_w1 = es.enter_context(nc.semaphore("s_w1"))
        s_wrow = es.enter_context(nc.semaphore("s_wrow"))
        s_x = es.enter_context(nc.semaphore("s_x"))
        s_p2 = [es.enter_context(nc.semaphore(f"s_p2_{i}")) for i in range(2)]
        s_wr2 = [es.enter_context(nc.semaphore(f"s_wr2_{i}")) for i in range(2)]
        s_wtr = es.enter_context(nc.semaphore("s_wtr"))
        s_tl2 = [es.enter_context(nc.semaphore(f"s_tl2_{i}")) for i in range(2)]
        s_d4 = [es.enter_context(nc.semaphore(f"s_d4_{i}")) for i in range(4)]
        s_go = es.enter_context(nc.semaphore("s_go"))
        s_v = es.enter_context(nc.semaphore("s_v"))
        s_s = es.enter_context(nc.semaphore("s_s"))
        s_t = es.enter_context(nc.semaphore("s_t"))
        s_g = es.enter_context(nc.semaphore("s_g"))

        def vinc(fn):
            n = E.inc(s_v, 1)
            def emit(v, fn=fn, n=n):
                v.wait_ge(s_v, n - 1)
                fn(v).then_inc(s_v, 1)
            E.p["vector"].append(emit)
            return n

        def sinc(fn):
            n = E.inc(s_s, 1)
            def emit(s, fn=fn, n=n):
                s.wait_ge(s_s, n - 1)
                fn(s).then_inc(s_s, 1)
            E.p["scalar"].append(emit)
            return n

        def tinc(fn):
            n = E.inc(s_t, 1)
            E.p["tensor"].append(lambda t: fn(t).then_inc(s_t, 1))
            return n

        def ginc(fn):
            n = E.inc(s_g, 1)
            def emit(g, fn=fn, n=n):
                g.wait_ge(s_g, n - 1)
                fn(g).then_inc(s_g, 1)
            E.p["gpsimd"].append(emit)
            return n

        def t_(fn):
            E.p["tensor"].append(fn)

        def dma(q, out, in_, sem=None, **kw):
            if sem is None:
                sem = s_ld if q == "sync" else s_go
            n = E.inc(sem, 16)
            E.p[q].append(lambda e: e.dma_start(out=out, in_=in_, **kw)
                          .then_inc(sem, 16))
            return n

        def wt(eng, sem, val):
            E.p[eng].append(lambda e: e.wait_ge(sem, val))

        def barrier(engs=("vector", "scalar", "tensor")):
            marks = {}
            for e in engs:
                sem = {"vector": s_v, "scalar": s_s, "tensor": s_t,
                       "gpsimd": s_g}[e]
                n = E.inc(sem, 1)
                def emit(x, sem=sem, n=n):
                    x.wait_ge(sem, n - 1)
                    x.nop().then_inc(sem, 1)
                E.p[e].append(emit)
                marks[e] = (sem, n)
            for e in engs:
                for o in engs:
                    if o != e:
                        wt(e, *marks[o])

        qd_ctr = [0]
        last_gf = [0]

        def qd_emit(X, OUT, F, gB, par=None, nb0=0):
            nb = F // 16
            a_s = qpool[:, nb0 * 16:nb0 * 16 + F]
            gA = qpool[:, I + nb0 * 16:I + nb0 * 16 + F]
            if par is None:
                par = qd_ctr[0] % 2
                qd_ctr[0] += 1
            bmx = qstat[:, par * 384 + nb0:par * 384 + nb0 + nb]
            rsc = qstat[:, par * 384 + 192 + nb0:par * 384 + 192 + nb0 + nb]
            vinc(lambda v: v.tensor_reduce(
                out=bmx, in_=X.rearrange("p (b k) -> p b k", k=16),
                axis=AXX, op=ALU.max, apply_absolute_value=True))
            vinc(lambda v: v.tensor_scalar(out=bmx, in0=bmx, scalar1=1.0 / 6.0,
                                           scalar2=1e-12, op0=ALU.mult, op1=ALU.max))
            vinc(lambda v: v.reciprocal(out=rsc, in_=bmx))
            vinc(lambda v: v.tensor_tensor(
                out=a_s.rearrange("p (b k) -> p b k", k=16),
                in0=X.rearrange("p (b k) -> p b k", k=16),
                in1=rsc.unsqueeze(2).broadcast_to((128, nb, 16)), op=ALU.mult))
            vinc(lambda v: v.tensor_tensor(out=a_s, in0=a_s, in1=a_s,
                                           op=ALU.mult))
            if nb0 == 0:
                wt("vector", s_g, last_gf[0])
            vinc(lambda v: v.tensor_scalar(out=gA, in0=a_s, scalar1=M2[0],
                                           scalar2=0.0, op0=ALU.is_gt, op1=ALU.add))
            for m2 in M2[1:4]:
                vinc(lambda v, m2=m2: v.scalar_tensor_tensor(
                    out=gA, in0=a_s, scalar=m2, in1=gA, op0=ALU.is_gt, op1=ALU.add))
            vinc(lambda v: v.tensor_scalar(out=gB, in0=a_s, scalar1=M2[6],
                                           scalar2=2.0, op0=ALU.is_gt, op1=ALU.mult))
            vinc(lambda v: v.scalar_tensor_tensor(
                out=gB, in0=a_s, scalar=M2[5], in1=gB, op0=ALU.is_gt, op1=ALU.add))
            vinc(lambda v: v.scalar_tensor_tensor(
                out=gB, in0=a_s, scalar=M2[4], in1=gB, op0=ALU.is_gt, op1=ALU.add))
            cmb = vinc(lambda v: v.scalar_tensor_tensor(
                out=gA, in0=gA, scalar=0.5, in1=gB, op0=ALU.mult, op1=ALU.add))
            wt("scalar", s_v, cmb)
            t2 = sinc(lambda s: s.activation(out=gB, in_=X, func=AF.Sign))
            wt("gpsimd", s_s, t2)
            ginc(lambda g: g.tensor_tensor(
                out=gB.rearrange("p (b k) -> p b k", k=16),
                in0=gB.rearrange("p (b k) -> p b k", k=16),
                in1=bmx.unsqueeze(2).broadcast_to((128, nb, 16)), op=ALU.mult))
            gf = ginc(lambda g: g.tensor_tensor(out=OUT, in0=gA, in1=gB,
                                                op=ALU.mult))
            last_gf[0] = gf
            return {"v": cmb, "s": t2, "g": gf}

        def ln_emit(SRC, DST, SQ, ntt):
            wt("scalar", s_v, E.val(s_v))
            mu = stats[:, 0:ntt]
            q2 = stats[:, 8:8 + ntt]
            var = stats[:, 16:16 + ntt]
            rstd = stats[:, 24:24 + ntt]
            vinc(lambda v: v.tensor_reduce(
                out=mu, in_=SRC.rearrange("p (tt d) -> p tt d", d=D),
                axis=AXX, op=ALU.add))
            t1 = sinc(lambda s: s.activation(out=SQ, in_=SRC, func=AF.Square))
            wt("vector", s_s, t1)
            vinc(lambda v: v.tensor_reduce(
                out=q2, in_=SQ.rearrange("p (tt d) -> p tt d", d=D),
                axis=AXX, op=ALU.add))
            vinc(lambda v: v.tensor_scalar(out=mu, in0=mu, scalar1=1.0 / D,
                                           scalar2=0.0, op0=ALU.mult, op1=ALU.add))
            vinc(lambda v: v.tensor_scalar(out=q2, in0=q2, scalar1=1.0 / D,
                                           scalar2=0.0, op0=ALU.mult, op1=ALU.add))
            vinc(lambda v: v.tensor_tensor(out=var, in0=mu, in1=mu, op=ALU.mult))
            t2 = vinc(lambda v: v.tensor_tensor(out=var, in0=q2, in1=var,
                                                op=ALU.subtract))
            wt("scalar", s_v, t2)
            sinc(lambda s: s.activation(out=rstd, in_=var, func=AF.Ln,
                                        bias=consts[:, 1:2]))
            t3 = sinc(lambda s: s.activation(out=rstd, in_=rstd, func=AF.Exp,
                                             scale=-0.5))
            wt("vector", s_s, t3)
            for tt in range(ntt):
                vinc(lambda v, tt=tt: v.tensor_scalar(
                    out=DST[:, tt * D:(tt + 1) * D],
                    in0=SRC[:, tt * D:(tt + 1) * D],
                    scalar1=mu[:, tt:tt + 1], scalar2=rstd[:, tt:tt + 1],
                    op0=ALU.subtract, op1=ALU.mult))

        ptr_mark = {}
        ptr_ctr = [0]

        def pe_tr(src, dst):
            slot = ptr_ctr[0] % 2
            ptr_ctr[0] += 1
            if slot in ptr_mark:
                wt("tensor", s_s, ptr_mark[slot])
            tn = tinc(lambda t, src=src, slot=slot: t.transpose(
                out=ptr[slot], in_=src, identity=ident))
            wt("scalar", s_t, tn)
            en = sinc(lambda s, dst=dst, slot=slot: s.activation(
                out=dst, in_=ptr[slot], func=AF.Copy))
            ptr_mark[slot] = en
            return en

        n0 = 0
        for dst, src in [(ident, ident_in), (consts, consts_in),
                         (acols, acols_in), (binc, binc_in), (bdtc, bdtc_in),
                         (dcol, dcol_in), (boutr, boutr_in), (gamr, gamr_in),
                         (betr, betr_in), (bbc_f[0:32, 0:1], bbc_in)]:
            n0 = dma("sync", dst, src, sem=s_w1)
        wt("scalar", s_w1, n0)
        wt("vector", s_w1, n0)
        t1 = sinc(lambda s: s.activation(out=acols, in_=acols, func=AF.Exp))
        wt("vector", s_s, t1)
        vinc(lambda v: v.tensor_scalar(out=acols, in0=acols, scalar1=-1.0,
                                       scalar2=0.0, op0=ALU.mult, op1=ALU.add))
        vinc(lambda v: v.tensor_scalar(out=binh, in0=binc, scalar1=0.5,
                                       scalar2=0.0, op0=ALU.mult, op1=ALU.add))

        def cast_weight(src_hbm, dst_scr, ncols):
            for gg in range(GT):
                sl = gg % 2
                nl = dma("sync", wstage[sl][:, 0:ncols],
                         src_hbm[gg * 128:(gg + 1) * 128, :], sem=s_wrow)
                wt("vector", s_wrow, nl)
                cn = vinc(lambda v, sl=sl, ncols=ncols: v.tensor_copy(
                    out=wqstage[sl][:, 0:ncols], in_=wstage[sl][:, 0:ncols]))
                wt("gpsimd", s_v, cn)
                gn = dma("gpsimd", dst_scr[gg * 128:(gg + 1) * 128, :],
                         wqstage[sl][:, 0:ncols])
                wt("sync", s_v, cn)
                wt("vector", s_go, gn)

        cast_weight(w_inT, s_winT, 2 * D)
        cast_weight(w_dtT, s_wdtT, D)
        cast_weight(w_outT, s_woutT, D)
        cast_weight(w_bcT, s_wbcT, 32)

        def prep_ffn(jobs):
            tiles = []
            for (src_hbm, dst_scr, nrows, ncols) in jobs:
                for rt in range(nrows // 128):
                    tiles.append((src_hbm, dst_scr, rt, ncols))
            loads = {}
            qds = {}
            loads[0] = dma("sync", wstage[0][:, 0:tiles[0][3]],
                           tiles[0][0][tiles[0][2] * 128:
                                       (tiles[0][2] + 1) * 128, :], sem=s_wr2[0])
            for i, (src_hbm, dst_scr, rt, ncols) in enumerate(tiles):
                sl = i % 2
                if i + 1 < len(tiles):
                    s2_, d2_, rt2, nc2 = tiles[i + 1]
                    if i - 1 in qds:
                        wt("sync", s_v, qds[i - 1]["v"])
                        wt("sync", s_s, qds[i - 1]["s"])
                        wt("sync", s_g, qds[i - 1]["g"])
                    loads[i + 1] = dma("sync", wstage[(i + 1) % 2][:, 0:nc2],
                                       s2_[rt2 * 128:(rt2 + 1) * 128, :],
                                       sem=s_wr2[(i + 1) % 2])
                wt("vector", s_wr2[sl], loads[i])
                wt("scalar", s_wr2[sl], loads[i])
                mkq = qd_emit(wstage[sl][:, 0:ncols], wqstage[sl][:, 0:ncols],
                              ncols, uprow[:, 0:ncols])
                qds[i] = mkq
                wt("sync", s_g, mkq["g"])
                wt("scalar", s_g, mkq["g"])
                trb = 0
                for cb in range(ncols // 128):
                    q_ = "sync"
                    trb = dma(q_, wq_ev[:, cb * 128:(cb + 1) * 128],
                              wqstage[sl][:, cb * 128:(cb + 1) * 128],
                              sem=s_wtr, transpose=True)
                wt("gpsimd", s_wtr, E.val(s_wtr))
                gn = 0
                for cb in range(ncols // 128):
                    gn = dma("gpsimd", dst_scr[cb * 128:(cb + 1) * 128,
                                               rt * 128:(rt + 1) * 128],
                             wq_ev[:, cb * 128:(cb + 1) * 128])
                wt("vector", s_wtr, E.val(s_wtr))
                wt("sync", s_go, gn)
                wt("scalar", s_go, gn)

        prep_ffn([(w_gate, s_wgT, I, D), (w_up, s_wuT, I, D),
                  (w_down, s_wdnT, D, I)])

        barrier()
        wt("sync", s_go, E.val(s_go))
        wt("gpsimd", s_v, E.val(s_v))

        nl = dma("sync", wout_r.rearrange("p (g d) -> p g d", g=GT),
                 s_woutT.rearrange("(g p) d -> p g d", p=128), sem=s_w1)
        nl = dma("sync", wbc_r.rearrange("p (g d) -> p g d", g=GT),
                 s_wbcT.rearrange("(g p) d -> p g d", p=128), sem=s_w1)
        wt("tensor", s_w1, nl)

        SKIP_P1 = bool(os.environ.get("K_SKIP_P1"))
        SKIP_P2 = bool(os.environ.get("K_SKIP_P2"))
        P1S = int(os.environ.get("K_P1STAGES", "99"))
        x_ch = x_hbm.rearrange("(c tt p) d -> c p tt d", p=128, tt=TT)
        xn_ch = s_xn.rearrange("(c tt p) d -> c p tt d", p=128, tt=TT)
        sg_ch = s_sg.rearrange("(c tt p) d -> c p tt d", p=128, tt=TT)
        ttd = lambda ap: ap.rearrange("p (tt d) -> p tt d", d=D)
        win_pan = s_winT.rearrange("(g p) o -> p g o", p=128)
        wdt_pan = s_wdtT.rearrange("(g p) o -> p g o", p=128)
        scan_marks = {}
        scan_i = [0]
        sc_hc = [0]
        uh_free = {}
        hb_free = {}
        fin_mark = {}
        ug_last = [0]
        xn_sp = [0]

        chunk_gate = [E.val(s_v)]
        for c in range(NCH if not SKIP_P1 else 0):
            par = c % 2
            wt("sync", s_v, chunk_gate[0])
            nx = dma("sync", ttd(xbuf), x_ch[c], sem=s_x)
            pan_cnt = {0: dma("sync", winr[:, 0:GT * 128],
                              win_pan[:, :, 0:128], sem=s_p2[0])}
            wt("vector", s_x, nx)
            wt("scalar", s_x, nx)
            wt("vector", s_go, E.val(s_go))

            ln_emit(xbuf, xn, xss, TT)
            xnv = vinc(lambda v: v.tensor_copy(out=xnb, in_=xn))
            wt("gpsimd", s_v, xnv)
            xn_sp[0] = dma("gpsimd", xn_ch[c], ttd(xn))
            wt("sync", s_v, xnv)
            trn = 0
            for tt in range(TT):
                for gg in range(GT):
                    trn = dma("sync",
                              xnT[:, gg * T + tt * 128: gg * T + (tt + 1) * 128],
                              xnb[:, tt * D + gg * 128: tt * D + (gg + 1) * 128],
                              sem=s_x, transpose=True)
            wt("tensor", s_x, trn)

            XZ_NJ = int(os.environ.get("K_XZ_NJ", "16"))
            XZ_MM = os.environ.get("K_XZ_MM", "1") == "1"
            XZ_EV = os.environ.get("K_XZ_EV", "1") == "1"
            last_mt = {}
            for j in range((XZ_NJ if P1S >= 2 else 0)):
                pm = p_mm[j % 2]
                if j + 1 < 16:
                    if j - 1 in last_mt:
                        wt("sync", s_t, last_mt[j - 1])
                    pan_cnt[j + 1] = dma(
                        "sync", winr[:, ((j + 1) % 2) * GT * 128:
                                     (((j + 1) % 2) + 1) * GT * 128],
                        win_pan[:, :, (j + 1) * 128:(j + 2) * 128],
                        sem=s_p2[(j + 1) % 2])
                wt("tensor", s_p2[j % 2], pan_cnt[j])
                for gg in range(GT if XZ_MM else 0):
                    def mm(t, j=j, gg=gg, pm=pm):
                        return t.matmul(
                            out=pm[:, 0:T],
                            lhsT=winr[:, (j % 2) * GT * 128 + gg * 128:
                                      (j % 2) * GT * 128 + (gg + 1) * 128],
                            rhs=xnT[:, gg * T:(gg + 1) * T],
                            start=(gg == 0), stop=(gg == GT - 1))
                    if gg == GT - 1:
                        mt = tinc(mm)
                    else:
                        t_(lambda t, f=mm: f(t))
                if not XZ_MM:
                    mt = E.inc(s_t, 1)
                    E.p["tensor"].append(lambda t: t.nop().then_inc(s_t, 1))
                last_mt[j] = mt
                wt("vector", s_t, mt)
                if not XZ_EV:
                    ev = vinc(lambda v: v.nop())
                    wt("tensor", s_v, ev)
                    continue
                if j < 8:
                    ev = vinc(lambda v, j=j, pm=pm: v.tensor_scalar(
                        out=xss[:, j * T:(j + 1) * T], in0=pm[:, 0:T],
                        scalar1=binc[:, j:j + 1], scalar2=0.0,
                        op0=ALU.add, op1=ALU.add))
                else:
                    jj = j - 8
                    ZL = int(os.environ.get("K_ZL", "3"))
                    ZNS = bool(os.environ.get("K_Z_NOSCAL"))
                    ev0 = vinc(lambda v, j=j, jj=jj, pm=pm: v.tensor_scalar(
                        out=zs[:, jj * T:(jj + 1) * T], in0=pm[:, 0:T],
                        scalar1=binc[:, j:j + 1], scalar2=0.0,
                        op0=ALU.add, op1=ALU.add))
                    ev = ev0
                    if ZL >= 2:
                        if ZNS:
                            ev = vinc(lambda v, jj=jj: v.tensor_copy(
                                out=ln_scr[:, 0:T],
                                in_=zs[:, jj * T:(jj + 1) * T]))
                        else:
                            wt("scalar", s_v, ev0)
                            th = sinc(lambda s, jj=jj: s.activation(
                                out=ln_scr[:, 0:T],
                                in_=zs[:, jj * T:(jj + 1) * T],
                                func=AF.Sigmoid))
                            wt("vector", s_s, th)
                    if ZL >= 3:
                        ev = vinc(lambda v, jj=jj: v.tensor_tensor(
                            out=zs[:, jj * T:(jj + 1) * T],
                            in0=zs[:, jj * T:(jj + 1) * T],
                            in1=ln_scr[:, 0:T], op=ALU.mult))
                    wt("scalar", s_v, ev)
                wt("tensor", s_v, ev)

            xssb = xnT
            if P1S >= 3:
                wt("vector", s_t, last_mt[15])
                csb = vinc(lambda v: v.tensor_copy(out=xssb, in_=xss))
                wt("tensor", s_v, csb)

            if P1S >= 3:
                wt("sync", s_t, last_mt[14])
                dt_cnt = {0: dma("sync", wdtr[:, 0:GT * 128],
                                 wdt_pan[:, :, 0:128], sem=s_p2[0])}
            for j in range(8 if P1S >= 3 else 0):
                pm = p_mm[j % 2]
                if j + 1 < 8:
                    if j - 1 in last_mt:
                        wt("sync", s_t, last_mt[j - 1])
                    else:
                        wt("sync", s_t, last_mt[15])
                    dt_cnt[j + 1] = dma(
                        "sync", wdtr[:, ((j + 1) % 2) * GT * 128:
                                     (((j + 1) % 2) + 1) * GT * 128],
                        wdt_pan[:, :, (j + 1) * 128:(j + 2) * 128],
                        sem=s_p2[(j + 1) % 2])
                wt("tensor", s_p2[j % 2], dt_cnt[j])
                for gg in range(GT):
                    def mm(t, j=j, gg=gg, pm=pm):
                        return t.matmul(
                            out=pm[:, 0:T],
                            lhsT=wdtr[:, (j % 2) * GT * 128 + gg * 128:
                                      (j % 2) * GT * 128 + (gg + 1) * 128],
                            rhs=xssb[:, gg * T:(gg + 1) * T],
                            start=(gg == 0), stop=(gg == GT - 1))
                    if gg == GT - 1:
                        mt = tinc(mm)
                    else:
                        t_(lambda t, f=mm: f(t))
                last_mt[j] = mt
                wt("vector", s_t, mt)
                xb = vinc(lambda v, j=j, pm=pm: v.tensor_scalar(
                    out=ln_scr[:, T:2 * T], in0=pm[:, 0:T],
                    scalar1=bdtc[:, j:j + 1], scalar2=0.0,
                    op0=ALU.add, op1=ALU.add))
                wt("scalar", s_v, xb)
                sinc(lambda s: s.activation(out=ln_scr[:, 0:T],
                                            in_=ln_scr[:, T:2 * T],
                                            func=AF.Abs))
                sinc(lambda s: s.activation(out=ln_scr[:, 0:T],
                                            in_=ln_scr[:, 0:T],
                                            func=AF.Exp, scale=-1.0))
                sinc(lambda s: s.activation(out=ln_scr[:, 0:T],
                                            in_=ln_scr[:, 0:T],
                                            func=AF.Ln, bias=consts[:, 0:1]))
                rl = sinc(lambda s: s.activation(
                    out=ln_scr[:, T:2 * T], in_=ln_scr[:, T:2 * T],
                    func=AF.Relu))
                wt("vector", s_s, rl)
                ev = vinc(lambda v, j=j: v.tensor_tensor(
                    out=dts[:, j * T:(j + 1) * T], in0=ln_scr[:, 0:T],
                    in1=ln_scr[:, T:2 * T], op=ALU.add))
                wt("tensor", s_v, ev)
                wt("scalar", s_v, ev)

            for gg in range(GT if P1S >= 4 else 0):
                def mm(t, gg=gg):
                    return t.matmul(
                        out=PSF[0:32, 0:T],
                        lhsT=wbc_r[:, gg * 32:(gg + 1) * 32],
                        rhs=xssb[:, gg * T:(gg + 1) * T],
                        start=(gg == 0), stop=(gg == GT - 1))
                if gg == GT - 1:
                    mt = tinc(mm)
                else:
                    t_(lambda t, f=mm: f(t))
            if P1S >= 4:
                wt("vector", s_t, mt)
                bcv = vinc(lambda v: v.tensor_scalar(
                    out=bcbf[0:32, :], in0=PSF[0:32, 0:T],
                    scalar1=bbc_f[0:32, 0:1], scalar2=0.0,
                    op0=ALU.add, op1=ALU.add))
                wt("tensor", s_v, bcv)
                wt("gpsimd", s_v, bcv)
                b1 = dma("gpsimd", s_bc[par], bcbf[0:32, :])
                wt("sync", s_go, b1)
                dma("sync", brep,
                    s_bc[par, 0:16].rearrange("n t -> (n t)").partition_broadcast(128),
                    sem=s_x)
                b3 = dma("sync", crep,
                         s_bc[par, 16:32].rearrange("n t -> (n t)").partition_broadcast(128),
                         sem=s_x)
                wt("vector", s_x, b3)
                bc_mt = mt

            wt("vector", s_go, E.val(s_go))
            wt("gpsimd", s_go, E.val(s_go))
            if P1S >= 4:
                wt("gpsimd", s_x, b3)
                wt("gpsimd", s_t, bc_mt)
            uhalf_b = [uhalf, xnT]
            hbuf_b = [hbuf, xnb]

            def emit_fin(gg):
                wt("vector", s_g, fin_mark[gg])
                vinc(lambda v, gg=gg: v.scalar_tensor_tensor(
                    out=dts[:, gg * T:(gg + 1) * T],
                    in0=xss[:, gg * T:(gg + 1) * T],
                    scalar=dcol[:, gg:gg + 1],
                    in1=dts[:, gg * T:(gg + 1) * T],
                    op0=ALU.mult, op1=ALU.add))
                vinc(lambda v, gg=gg: v.tensor_tensor(
                    out=y2b[:, gg * T:(gg + 1) * T],
                    in0=dts[:, gg * T:(gg + 1) * T],
                    in1=zs[:, gg * T:(gg + 1) * T], op=ALU.mult))

            for gg in range(GT if P1S >= 5 else 0):
                wt("vector", s_g, ug_last[0])
                dx = vinc(lambda v, gg=gg: v.tensor_tensor(
                    out=dtxb, in0=dts[:, gg * T:(gg + 1) * T],
                    in1=xss[:, gg * T:(gg + 1) * T], op=ALU.mult))
                for half in range(2):
                    p = sc_hc[0] % 2
                    sc_hc[0] += 1
                    ub = uhalf_b[p]
                    hb = hbuf_b[p]
                    wt("gpsimd", s_v, dx)
                    if p in uh_free:
                        wt("gpsimd", s_v, uh_free[p])
                    ug = ginc(lambda g, ub=ub, half=half: g.tensor_tensor(
                        out=ub.rearrange("p (n t) -> p n t", n=8),
                        in0=dtxb.unsqueeze(1).broadcast_to((128, 8, T)),
                        in1=brep[:, half * 8 * T:(half * 8 + 8) * T]
                        .rearrange("p (n t) -> p n t", n=8),
                        op=ALU.mult))
                    ug_last[0] = ug
                    if p in hb_free:
                        wt("vector", s_g, hb_free[p])
                    wt("vector", s_g, ug)
                    for k in range(8):
                        n_ = half * 8 + k
                        i_ = scan_i[0]
                        slot = i_ % 4
                        if i_ >= 4:
                            wt("scalar", s_v, scan_marks[i_ - 4])
                        da = sinc(lambda s, gg=gg, n_=n_, slot=slot:
                                  s.activation(
                                      out=dA[:, slot * T:(slot + 1) * T],
                                      in_=dts[:, gg * T:(gg + 1) * T],
                                      func=AF.Exp,
                                      scale=acols[:, gg * 16 + n_:
                                                  gg * 16 + n_ + 1]))
                        wt("vector", s_s, da)
                        if c == 0:
                            init = 0.0
                        else:
                            init = carry[:, (1 - par) * 128 + gg * 16 + n_:
                                         (1 - par) * 128 + gg * 16 + n_ + 1]
                        mk = vinc(lambda v, k=k, slot=slot, init=init, hb=hb,
                                  ub=ub:
                                  v.tensor_tensor_scan(
                                      out=hb[:, k * T:(k + 1) * T],
                                      data0=dA[:, slot * T:(slot + 1) * T],
                                      data1=ub[:, k * T:(k + 1) * T],
                                      initial=init, op0=ALU.mult, op1=ALU.add))
                        scan_marks[i_] = mk
                        scan_i[0] += 1
                    uh_free[p] = E.val(s_v)
                    cv = vinc(lambda v, gg=gg, half=half, par=par, hb=hb:
                              v.tensor_copy(
                        out=carry[:, par * 128 + gg * 16 + half * 8:
                                  par * 128 + gg * 16 + half * 8 + 8],
                        in_=hb[:, T - 1::T]))
                    wt("gpsimd", s_v, cv)
                    ginc(lambda g, half=half, hb=hb: g.tensor_tensor(
                        out=hb, in0=hb,
                        in1=crep[:, half * 8 * T:(half * 8 + 8) * T],
                        op=ALU.mult))
                    w = 8
                    while w > 1:
                        w //= 2
                        ginc(lambda g, w=w, hb=hb: g.tensor_tensor(
                            out=hb[:, 0:w * T], in0=hb[:, 0:w * T],
                            in1=hb[:, w * T:2 * w * T], op=ALU.add))
                    if half == 0:
                        hb_free[p] = ginc(lambda g, hb=hb: g.tensor_copy(
                            out=ln_scr[:, 0:T], in_=hb[:, 0:T]))
                    else:
                        hb_free[p] = ginc(lambda g, gg=gg, hb=hb:
                                          g.tensor_tensor(
                            out=dts[:, gg * T:(gg + 1) * T],
                            in0=hb[:, 0:T], in1=ln_scr[:, 0:T], op=ALU.add))
                        fin_mark[gg] = hb_free[p]
                if gg:
                    emit_fin(gg - 1)
            if P1S >= 5:
                emit_fin(GT - 1)
            barrier(("vector", "scalar", "tensor", "gpsimd"))

            if P1S >= 6:
                wt("vector", s_go, xn_sp[0])
                vinc(lambda v: v.tensor_tensor(
                    out=ttd(xn), in0=ttd(xn),
                    in1=boutr.unsqueeze(1).broadcast_to((128, TT, D)),
                    op=ALU.add))
            for tt in range(TT if P1S >= 6 else 0):
                for dh in range(2):
                    pm = p_mm[dh]
                    for gg in range(GT):
                        def mm(t, tt=tt, gg=gg, dh=dh, pm=pm):
                            return t.matmul(
                                out=pm[:, 0:512],
                                lhsT=y2b[:, gg * T + tt * 128:
                                         gg * T + (tt + 1) * 128],
                                rhs=wout_r[:, gg * D + dh * 512:
                                           gg * D + (dh + 1) * 512],
                                start=(gg == 0), stop=(gg == GT - 1))
                        if gg == GT - 1:
                            mt = tinc(mm)
                        else:
                            t_(lambda t, f=mm: f(t))
                    wt("vector", s_t, mt)
                    ev = vinc(lambda v, tt=tt, dh=dh, pm=pm:
                              v.scalar_tensor_tensor(
                        out=pre[:, tt * D + dh * 512: tt * D + (dh + 1) * 512],
                        in0=pm[:, 0:512], scalar=0.0,
                        in1=xn[:, tt * D + dh * 512: tt * D + (dh + 1) * 512],
                        op0=ALU.add, op1=ALU.add))
                    wt("tensor", s_v, ev)

            if P1S >= 7:
                ln_emit(pre, pre, xss, TT)
            for tt in range(TT if P1S >= 7 else 0):
                vinc(lambda v, tt=tt: v.tensor_tensor(
                    out=pre[:, tt * D:(tt + 1) * D],
                    in0=pre[:, tt * D:(tt + 1) * D], in1=gamr, op=ALU.mult))
                vinc(lambda v, tt=tt: v.tensor_tensor(
                    out=pre[:, tt * D:(tt + 1) * D],
                    in0=pre[:, tt * D:(tt + 1) * D], in1=betr, op=ALU.add))
            if P1S >= 7:
                sv = vinc(lambda v: v.tensor_copy(out=xnb, in_=pre))
                wt("gpsimd", s_v, sv)
                dma("gpsimd", sg_ch[c], ttd(xnb))
            barrier()
            chunk_gate[0] = E.val(s_v)

        wt("sync", s_go, E.val(s_go))

        wt("sync", s_v, chunk_gate[0])
        ng = dma("sync", wg_r.rearrange("p (g i) -> p g i", g=GT),
                 s_wgT.rearrange("(g p) i -> p g i", p=128), sem=s_w1)
        wt("tensor", s_w1, ng)
        wu_pan = s_wuT.rearrange("(g p) i -> p g i", p=128)
        x_tl = x_hbm.rearrange("(q p) d -> q p d", p=128)
        xn_tl = s_xn.rearrange("(q p) d -> q p d", p=128)
        sg_tl = s_sg.rearrange("(q p) d -> q p d", p=128)
        out_tl = out_hbm.rearrange("(q p) d -> q p d", p=128)
        up_mark = {}
        up_ld = {}
        dn_mark = {}
        pgu_free = {}
        pgu_ctr = [0]
        dn_evac_prev = [0]
        tr_last = [0]
        tl_ld = {}
        tl_free = {}

        def up_load(gsec):
            sec = gsec % 6
            return dma("scalar",
                       wur[:, (gsec % 2) * GT * 512:
                           (gsec % 2) * GT * 512 + GT * SECW[sec]],
                       wu_pan[:, :, SECO[sec]:SECO[sec] + SECW[sec]],
                       sem=s_p2[gsec % 2])

        def p2_loads(qq, par2):
            sm = s_tl2[par2]
            dma("gpsimd", xn2[:, par2 * 2 * D:par2 * 2 * D + D],
                xn_tl[2 * qq], sem=sm)
            dma("gpsimd", xn2[:, par2 * 2 * D + D:(par2 + 1) * 2 * D],
                xn_tl[2 * qq + 1], sem=sm)
            dma("gpsimd", s2[:, par2 * 2 * D:par2 * 2 * D + D],
                sg_tl[2 * qq], sem=sm)
            return dma("gpsimd", s2[:, par2 * 2 * D + D:(par2 + 1) * 2 * D],
                       sg_tl[2 * qq + 1], sem=sm)

        if NPAIR and not SKIP_P2:
            tl_ld[0] = p2_loads(0, 0)
        pair_mk = {}
        xtr_last = [0]
        dn_v_last = [0]

        def pre_emit(qq):
            par2 = qq % 2
            wt("vector", s_tl2[par2], tl_ld[qq])
            wt("scalar", s_tl2[par2], tl_ld[qq])
            wt("gpsimd", s_t, xtr_last[0])
            pair_mk[qq] = qd_emit(xn2[:, par2 * 2 * D:(par2 + 1) * 2 * D],
                                  xnq, 2 * D, uprow[:, 0:2 * D])

        if NPAIR and not SKIP_P2:
            pre_emit(0)
        for q in range(NPAIR if not SKIP_P2 else 0):
            q0, q1 = 2 * q, 2 * q + 1
            par2 = q % 2
            s2v = s2[:, par2 * 2 * D:(par2 + 1) * 2 * D]
            mk = pair_mk[q]
            if q + 1 < NPAIR:
                if q - 1 in tl_free:
                    wt("gpsimd", s_v, tl_free[q - 1][0])
                    wt("gpsimd", s_s, tl_free[q - 1][1])
                tl_ld[q + 1] = p2_loads(q + 1, (q + 1) % 2)
            wt("sync", s_v, tl_free.get(q - 1, (0, 0))[0])
            nx2 = dma("sync", x2[:, 0:D], x_tl[q0], sem=s_x)
            nx2 = dma("sync", x2[:, D:2 * D], x_tl[q1], sem=s_x)
            for pre_s in (0, 1):
                gsec = q * 6 + pre_s
                if gsec - 2 in up_mark:
                    wt("scalar", s_t, up_mark[gsec - 2])
                up_ld[gsec] = up_load(gsec)
            wt("scalar", s_go, E.val(s_go))

            wt("tensor", s_g, mk["g"])
            for ti in range(2):
                for gg in range(GT):
                    pe_tr(xnq[:, ti * D + gg * 128: ti * D + (gg + 1) * 128],
                          xnqT[:, gg * 256 + ti * 128: gg * 256 + (ti + 1) * 128])
            xtr_last[0] = E.val(s_t)
            xnqT_done = E.val(s_s)

            for sec in range(6):
                w_ = SECW[sec]
                gsec = q * 6 + sec
                uslot = gsec % 2
                wt("tensor", s_p2[uslot], up_ld[gsec])
                if sec == 0:
                    wt("tensor", s_s, xnqT_done)
                    wt("scalar", s_v, dn_v_last[0])
                for ti in range(2):
                    for which in range(2):
                        pslot = pgu_ctr[0] % 2
                        ps = pgu[pslot]
                        pgu_ctr[0] += 1
                        if pslot in pgu_free:
                            wt("tensor", s_s, pgu_free[pslot])
                        for gg in range(GT):
                            if which == 0:
                                def mmf(t, ti=ti, sec=sec, gg=gg, w_=w_, ps=ps):
                                    return t.matmul(
                                        out=ps[:, 0:w_],
                                        lhsT=xnqT[:, gg * 256 + ti * 128:
                                                  gg * 256 + (ti + 1) * 128],
                                        rhs=wg_r[:, gg * I + SECO[sec]:
                                                 gg * I + SECO[sec] + w_],
                                        start=(gg == 0), stop=(gg == GT - 1))
                            else:
                                def mmf(t, ti=ti, gg=gg, w_=w_, uslot=uslot,
                                        ps=ps):
                                    return t.matmul(
                                        out=ps[:, 0:w_],
                                        lhsT=xnqT[:, gg * 256 + ti * 128:
                                                  gg * 256 + (ti + 1) * 128],
                                        rhs=wur[:, uslot * GT * 512 + gg * w_:
                                                uslot * GT * 512 + (gg + 1) * w_],
                                        start=(gg == 0), stop=(gg == GT - 1))
                            if gg == GT - 1:
                                mt = tinc(mmf)
                            else:
                                t_(lambda t, f=mmf: f(t))
                        wt("scalar", s_t, mt)
                        dstb = hstage if which == 0 else uprow
                        pgu_free[pslot] = sinc(
                            lambda s, ps=ps, dstb=dstb, ti=ti, sec=sec, w_=w_:
                            s.activation(out=dstb[:, ti * I + SECO[sec]:
                                                  ti * I + SECO[sec] + w_],
                                         in_=ps[:, 0:w_], func=AF.Copy))
                        if which == 1:
                            up_mark[gsec] = mt
                if sec + 2 < 6:
                    wt("scalar", s_t, up_mark[gsec])
                    up_ld[gsec + 2] = up_load(gsec + 2)

            SLC = [(0, 768), (768, 768), (1536, 768), (2304, 512)]
            dn_ready = {}
            prev_g = mk["g"]
            qd_h1 = None
            wt("gpsimd", s_t, tr_last[0])
            for ti in range(2):
                hsv = hstage[:, ti * I:(ti + 1) * I]
                upv = uprow[:, ti * I:(ti + 1) * I]
                hqv = hq[:, ti * I:(ti + 1) * I]
                parh = qd_ctr[0] % 2
                qd_ctr[0] += 1
                wt("scalar", s_g, prev_g)
                sigm = {}
                for (c0, cw) in SLC:
                    sigm[c0] = sinc(lambda s, hsv=hsv, c0=c0, cw=cw:
                                    s.activation(
                        out=qpool[:, c0:c0 + cw], in_=hsv[:, c0:c0 + cw],
                        func=AF.Sigmoid))
                for (c0, cw) in SLC:
                    wt("vector", s_s, sigm[c0])
                    vinc(lambda v, hsv=hsv, c0=c0, cw=cw: v.tensor_tensor(
                        out=hsv[:, c0:c0 + cw], in0=hsv[:, c0:c0 + cw],
                        in1=qpool[:, c0:c0 + cw], op=ALU.mult))
                    vinc(lambda v, hsv=hsv, upv=upv, c0=c0, cw=cw:
                         v.tensor_tensor(
                             out=hsv[:, c0:c0 + cw], in0=hsv[:, c0:c0 + cw],
                             in1=upv[:, c0:c0 + cw], op=ALU.mult))
                    mkh = qd_emit(hsv[:, c0:c0 + cw], hqv[:, c0:c0 + cw],
                                  cw, upv[:, c0:c0 + cw],
                                  par=parh, nb0=c0 // 16)
                    wt("tensor", s_g, mkh["g"])
                    for ib in range(c0 // 128, (c0 + cw) // 128):
                        en = pe_tr(hqv[:, ib * 128:(ib + 1) * 128],
                                   hqT[:, ib * 256 + ti * 128:
                                       ib * 256 + (ti + 1) * 128])
                        if ti == 1:
                            dn_ready[ib] = en
                prev_g = mkh["g"]
                qd_h1 = mkh
            tr_last[0] = E.val(s_t)

            wt("tensor", s_s, dn_evac_prev[0])
            for ib in range(NIB):
                slot = ib % 2
                gib = q * NIB + ib
                if gib - 2 in dn_mark:
                    wt("sync", s_t, dn_mark[gib - 2])
                nd = dma("sync", dnr[:, slot * D:(slot + 1) * D],
                         s_wdnT[ib * 128:(ib + 1) * 128, :], sem=s_d4[slot])
                wt("tensor", s_d4[slot], nd)
                wt("tensor", s_s, dn_ready[ib])
                for ti in range(2):
                    for dh in range(2):
                        def mmd(t, ib=ib, ti=ti, dh=dh, slot=slot):
                            return t.matmul(
                                out=p_dn[ti * 2 + dh][:, 0:512],
                                lhsT=hqT[:, ib * 256 + ti * 128:
                                         ib * 256 + (ti + 1) * 128],
                                rhs=dnr[:, slot * D + dh * 512:
                                        slot * D + (dh + 1) * 512],
                                start=(ib == 0), stop=(ib == NIB - 1))
                        if ti == 1 and dh == 1:
                            mt = tinc(mmd)
                        else:
                            t_(lambda t, f=mmd: f(t))
                dn_mark[gib] = mt
            if q + 1 < NPAIR:
                pre_emit(q + 1)
            wt("scalar", s_t, mt)
            wt("scalar", s_g, qd_h1["g"])
            ev_marks = []
            it = 0
            for ti in range(2):
                wt("vector", s_go, E.val(s_go))
                for dh in range(2):
                    pm = p_dn[ti * 2 + dh]
                    p_ = it % 2
                    g0 = gsc[:, p_ * 1024:p_ * 1024 + 512]
                    g1 = gsc[:, p_ * 1024 + 512:(p_ + 1) * 1024]
                    if it >= 2:
                        wt("scalar", s_v, ev_marks[it - 2])
                    sinc(lambda s, pm=pm, g0=g0: s.activation(
                        out=g0, in_=pm[:, 0:512], func=AF.Copy))
                    th = sinc(lambda s, g0=g0, g1=g1: s.activation(
                        out=g1, in_=g0, func=AF.Sigmoid))
                    wt("vector", s_s, th)
                    vinc(lambda v, g0=g0, g1=g1: v.tensor_tensor(
                        out=g0, in0=g0, in1=g1, op=ALU.mult))
                    o0 = ti * D + dh * 512
                    wt("vector", s_x, nx2)
                    vinc(lambda v, o0=o0, dh=dh, s2v=s2v, g0=g0:
                         v.tensor_tensor(
                        out=obuf[:, dh * 512:(dh + 1) * 512], in0=g0,
                        in1=s2v[:, o0:o0 + 512], op=ALU.mult))
                    ev = vinc(lambda v, o0=o0, dh=dh: v.tensor_tensor(
                        out=obuf[:, dh * 512:(dh + 1) * 512],
                        in0=obuf[:, dh * 512:(dh + 1) * 512],
                        in1=x2[:, o0:o0 + 512], op=ALU.add))
                    ev_marks.append(ev)
                    it += 1
                ov = E.val(s_v)
                wt("gpsimd", s_v, ov)
                dma("gpsimd", out_tl[q0 + ti], obuf)
            dn_evac_prev[0] = E.val(s_s)
            dn_v_last[0] = ov
            tl_free[q] = (ov, mk["s"])

        E.p["gpsimd"].append(lambda g: g.wait_ge(s_go, E.val(s_go)))

        block = es.enter_context(nc.Block())

        @block.sync
        def _(sy):
            for fn in E.p["sync"]:
                fn(sy)

        @block.vector
        def _(v):
            for fn in E.p["vector"]:
                fn(v)

        @block.scalar
        def _(s):
            for fn in E.p["scalar"]:
                fn(s)

        @block.gpsimd
        def _(g):
            for fn in E.p["gpsimd"]:
                fn(g)

        @block.tensor
        def _(t):
            for fn in E.p["tensor"]:
                fn(t)

    return nc


def _host_layouts(inputs):
    f32 = np.float32
    g = lambda k: np.asarray(inputs[k], f32)
    w_in = g("L0_attn_w_in"); b_in = g("L0_attn_b_in")
    w_dt = g("L0_attn_w_dt"); b_dt = g("L0_attn_b_dt")
    A_log = g("L0_attn_A_log")
    w_B = g("L0_attn_w_B"); b_B = g("L0_attn_b_B")
    w_C = g("L0_attn_w_C"); b_C = g("L0_attn_b_C")
    Dv = g("L0_attn_D")
    w_out = g("L0_attn_w_out"); b_out = g("L0_attn_b_out")
    gamma = g("L0_attn_gamma"); beta = g("L0_attn_beta")
    w_gate = g("L0_ffn_w_gate"); w_up = g("L0_ffn_w_up")
    w_dn = g("L0_ffn_w_down")

    bf = ml_dtypes.bfloat16
    consts = np.zeros((128, 8), f32)
    consts[:, 0] = 1.0
    consts[:, 1] = LN_EPS
    return {
        "w_inT": np.ascontiguousarray(w_in.T),
        "w_dtT": np.ascontiguousarray(w_dt.T),
        "w_outT": np.ascontiguousarray(w_out.T),
        "w_bcT": np.ascontiguousarray(np.concatenate([w_B.T, w_C.T], axis=1)),
        "w_gate": np.ascontiguousarray(w_gate),
        "w_up": np.ascontiguousarray(w_up),
        "w_down": np.ascontiguousarray(w_dn),
        "a_cols_log": np.ascontiguousarray(
            A_log.reshape(8, 128, NST).transpose(1, 0, 2).reshape(128, 128)),
        "b_in_cols": np.ascontiguousarray(b_in.reshape(16, 128).T),
        "b_dt_cols": np.ascontiguousarray(b_dt.reshape(8, 128).T),
        "bbc_col": np.ascontiguousarray(
            np.concatenate([b_B, b_C]).reshape(32, 1)),
        "d_cols": np.ascontiguousarray(Dv.reshape(8, 128).T),
        "b_out_rep": np.ascontiguousarray(
            np.broadcast_to(b_out, (128, D))).astype(bf),
        "gamma_rep": np.ascontiguousarray(
            np.broadcast_to(gamma, (128, D))).astype(bf),
        "beta_rep": np.ascontiguousarray(
            np.broadcast_to(beta, (128, D))).astype(bf),
        "ident_bf": np.eye(128, dtype=bf),
        "consts": consts,
    }


def kernel(**inputs):
    x = np.asarray(inputs["L0_attn_x"], np.float32)
    B, L, _ = x.shape
    shared = _host_layouts(inputs)
    nc = build_program(L=L, T=512)
    in_maps = []
    for b in range(B):
        m = dict(shared)
        m["x_b"] = np.ascontiguousarray(x[b])
        in_maps.append(m)
    res = run_bass_kernel_spmd(nc, in_maps, core_ids=list(range(B)))
    return np.stack([np.asarray(res.results[b]["out_b"]) for b in range(B)],
                    axis=0)


# version 10
# speedup vs baseline: 1.2257x; 1.2257x over previous
import contextlib
import os
import numpy as np
import ml_dtypes
import concourse.bass as bass
import concourse.mybir as mybir
from concourse.bass_utils import run_bass_kernel_spmd

F32 = mybir.dt.float32
BF16 = mybir.dt.bfloat16
AF = mybir.ActivationFunctionType
ALU = mybir.AluOpType
AXX = mybir.AxisListType.X

D = 1024
NST = 16
I = 2816
GT = 8
NIB = I // 128
SECW = [512, 512, 512, 512, 512, 256]
SECO = [0, 512, 1024, 1536, 2048, 2560]
LN_EPS = 1e-5
M2 = [0.0625, 0.5625, 1.5625, 3.0625, 6.25, 12.25, 25.0]
M1 = [0.25, 0.75, 1.25, 1.75, 2.5, 3.5, 5.0]

PF_COLS = 24192
PB_COLS = 53504


class Plan:
    def __init__(self):
        self.p = {e: [] for e in ("sync", "vector", "scalar", "gpsimd", "tensor")}
        self.tot = {}

    def inc(self, sem, k):
        key = id(sem)
        self.tot[key] = self.tot.get(key, 0) + k
        return self.tot[key]

    def val(self, sem):
        return self.tot.get(id(sem), 0)


def build_program(L=8192, T=512):
    NCH = L // T
    TT = T // 128
    NTIL = L // 128
    NPAIR = NTIL // 2

    nc = bass.Bass("TRN2", target_bir_lowering=False)

    def din(name, shape, dtype=F32):
        return nc.dram_tensor(name, shape, dtype, kind="ExternalInput").ap()

    x_hbm = din("x_b", [L, D])
    w_inT = din("w_inT", [D, 2 * D])
    w_dtT = din("w_dtT", [D, D])
    w_outT = din("w_outT", [D, D])
    w_bcT = din("w_bcT", [D, 32])
    w_gate = din("w_gate", [I, D])
    w_up = din("w_up", [I, D])
    w_down = din("w_down", [D, I])
    acols_in = din("a_cols_log", [128, 128])
    binc_in = din("b_in_cols", [128, 16])
    bdtc_in = din("b_dt_cols", [128, 8])
    bbc_in = din("bbc_col", [32, 1])
    dcol_in = din("d_cols", [128, 8])
    boutr_in = din("b_out_rep", [128, D], BF16)
    gamr_in = din("gamma_rep", [128, D], BF16)
    betr_in = din("beta_rep", [128, D], BF16)
    ident_in = din("ident_bf", [128, 128], BF16)
    consts_in = din("consts", [128, 8])

    out_hbm = nc.dram_tensor("out_b", [L, D], F32, kind="ExternalOutput").ap()

    def scr(name, shape, dtype):
        return nc.dram_tensor(name, shape, dtype, kind="Internal").ap()

    s_winT = scr("s_winT", [D, 2 * D], BF16)
    s_wdtT = scr("s_wdtT", [D, D], BF16)
    s_woutT = scr("s_woutT", [D, D], BF16)
    s_wbcT = scr("s_wbcT", [D, 32], BF16)
    s_wgT = scr("s_wgT", [D, I], BF16)
    s_wuT = scr("s_wuT", [D, I], BF16)
    s_wdnT = scr("s_wdnT", [I, D], BF16)
    s_xn = scr("s_xn", [L, D], F32)
    s_sg = scr("s_sg", [L, D], BF16)
    s_bc = scr("s_bc", [2, 32, T], BF16)

    E = Plan()
    es = contextlib.ExitStack()
    with es:
        PF = es.enter_context(nc.sbuf_tensor("PF", [128, PF_COLS], F32))
        PB = es.enter_context(nc.sbuf_tensor("PB", [128, PB_COLS], BF16))

        class Alloc:
            def __init__(self, pool, limit):
                self.pool, self.limit, self.off = pool, limit, 0

            def take(self, n):
                a = self.off
                self.off += n
                assert self.off <= self.limit, f"pool overflow {self.off}>{self.limit}"
                return self.pool[:, a:a + n]

        af = Alloc(PF, PF_COLS)
        ab = Alloc(PB, PB_COLS)

        consts = af.take(8)
        acols = af.take(128)
        binc = af.take(16)
        binh = af.take(16)
        bdtc = af.take(8)
        dcol = af.take(8)
        stats = af.take(64)
        bbc_f = af.take(32)
        ident = ab.take(128)
        boutr = ab.take(D)

        f_base = af.off
        b_base = ab.off

        xbuf = af.take(TT * D)
        pre = xbuf
        xn = af.take(TT * D)
        xss = af.take(GT * T)
        zs = af.take(GT * T)
        dts = af.take(GT * T)
        dA = af.take(4 * T)
        carry = af.take(2 * 128)
        ln_scr = af.take(2 * T)
        p1_f_end = af.off

        xnb = ab.take(TT * D)
        xnT = ab.take(GT * T)
        y2b = ab.take(GT * T)
        brep = ab.take(NST * T)
        crep = ab.take(NST * T)
        uhalf = ab.take(8 * T)
        hbuf = ab.take(8 * T)
        dtxb = ab.take(T)
        winr = ab.take(2 * GT * 128)
        wdtr = ab.take(2 * GT * 128)
        wout_r = ab.take(GT * D)
        wbc_r = ab.take(GT * 32)
        gamr = ab.take(D)
        betr = ab.take(D)
        p1_b_end = ab.off
        bcbf = hbuf[:, 0:T]

        af.off = f_base
        ab.off = b_base
        xn2 = af.take(4 * D)
        x2 = af.take(2 * D)
        qpool = af.take(2 * I)
        qstat = af.take(768)
        hstage = af.take(2 * I)
        uprow = af.take(2 * I)
        gsc = hstage[:, 0:2048]
        obuf = hstage[:, I:I + D]
        assert af.off <= p1_f_end, af.off

        wg_r = ab.take(GT * I)
        wur = ab.take(2 * GT * 512)
        dnr = ab.take(2 * D)
        xnq = ab.take(2 * D)
        xnqT = ab.take(2 * GT * 128)
        hq = ab.take(2 * I)
        hqT = ab.take(2 * NIB * 128)
        s2 = ab.take(4 * D)
        assert ab.off <= max(p1_b_end, ab.off)
        af.off = max(af.off, p1_f_end)
        ab.off = max(ab.off, p1_b_end)

        wstage = [hstage[:, 0:I], hstage[:, I:2 * I]]
        wqstage = [wg_r[:, 0:I], wg_r[:, I:2 * I]]
        wq_ev = wg_r[:, 2 * I:4 * I]

        PSF = es.enter_context(nc.psum_tensor("PSF", [128, 3072], F32))
        PSB = es.enter_context(nc.psum_tensor("PSB", [128, 2048], BF16))
        p_dn = [PSF[:, k * 512:(k + 1) * 512] for k in range(4)]
        pgu = [PSF[:, 2048:2560], PSF[:, 2560:3072]]
        p_mm = [pgu[0], pgu[1]]
        ptr = [PSB[:, 0:128], PSB[:, 1024:1152]]

        s_ld = es.enter_context(nc.semaphore("s_ld"))
        s_w1 = es.enter_context(nc.semaphore("s_w1"))
        s_wrow = es.enter_context(nc.semaphore("s_wrow"))
        s_x = es.enter_context(nc.semaphore("s_x"))
        s_p2 = [es.enter_context(nc.semaphore(f"s_p2_{i}")) for i in range(2)]
        s_wr2 = [es.enter_context(nc.semaphore(f"s_wr2_{i}")) for i in range(2)]
        s_wtr = es.enter_context(nc.semaphore("s_wtr"))
        s_tl2 = [es.enter_context(nc.semaphore(f"s_tl2_{i}")) for i in range(2)]
        s_d4 = [es.enter_context(nc.semaphore(f"s_d4_{i}")) for i in range(4)]
        s_go = es.enter_context(nc.semaphore("s_go"))
        s_v = es.enter_context(nc.semaphore("s_v"))
        s_s = es.enter_context(nc.semaphore("s_s"))
        s_t = es.enter_context(nc.semaphore("s_t"))
        s_g = es.enter_context(nc.semaphore("s_g"))

        def vinc(fn):
            n = E.inc(s_v, 1)
            def emit(v, fn=fn, n=n):
                v.wait_ge(s_v, n - 1)
                fn(v).then_inc(s_v, 1)
            E.p["vector"].append(emit)
            return n

        def sinc(fn):
            n = E.inc(s_s, 1)
            def emit(s, fn=fn, n=n):
                s.wait_ge(s_s, n - 1)
                fn(s).then_inc(s_s, 1)
            E.p["scalar"].append(emit)
            return n

        def tinc(fn):
            n = E.inc(s_t, 1)
            E.p["tensor"].append(lambda t: fn(t).then_inc(s_t, 1))
            return n

        def ginc(fn):
            n = E.inc(s_g, 1)
            def emit(g, fn=fn, n=n):
                g.wait_ge(s_g, n - 1)
                fn(g).then_inc(s_g, 1)
            E.p["gpsimd"].append(emit)
            return n

        def t_(fn):
            E.p["tensor"].append(fn)

        def dma(q, out, in_, sem=None, **kw):
            if sem is None:
                sem = s_ld if q == "sync" else s_go
            n = E.inc(sem, 16)
            E.p[q].append(lambda e: e.dma_start(out=out, in_=in_, **kw)
                          .then_inc(sem, 16))
            return n

        def wt(eng, sem, val):
            E.p[eng].append(lambda e: e.wait_ge(sem, val))

        def barrier(engs=("vector", "scalar", "tensor")):
            marks = {}
            for e in engs:
                sem = {"vector": s_v, "scalar": s_s, "tensor": s_t,
                       "gpsimd": s_g}[e]
                n = E.inc(sem, 1)
                def emit(x, sem=sem, n=n):
                    x.wait_ge(sem, n - 1)
                    x.nop().then_inc(sem, 1)
                E.p[e].append(emit)
                marks[e] = (sem, n)
            for e in engs:
                for o in engs:
                    if o != e:
                        wt(e, *marks[o])

        qd_ctr = [0]
        last_gf = [0]

        def qd_emit(X, OUT, F, gB, par=None, nb0=0):
            nb = F // 16
            a_s = qpool[:, nb0 * 16:nb0 * 16 + F]
            gA = qpool[:, I + nb0 * 16:I + nb0 * 16 + F]
            if par is None:
                par = qd_ctr[0] % 2
                qd_ctr[0] += 1
            bmx = qstat[:, par * 384 + nb0:par * 384 + nb0 + nb]
            rsc = qstat[:, par * 384 + 192 + nb0:par * 384 + 192 + nb0 + nb]
            vinc(lambda v: v.tensor_reduce(
                out=bmx, in_=X.rearrange("p (b k) -> p b k", k=16),
                axis=AXX, op=ALU.max, apply_absolute_value=True))
            vinc(lambda v: v.tensor_scalar(out=bmx, in0=bmx, scalar1=1.0 / 6.0,
                                           scalar2=1e-12, op0=ALU.mult, op1=ALU.max))
            vinc(lambda v: v.reciprocal(out=rsc, in_=bmx))
            vinc(lambda v: v.tensor_tensor(
                out=a_s.rearrange("p (b k) -> p b k", k=16),
                in0=X.rearrange("p (b k) -> p b k", k=16),
                in1=rsc.unsqueeze(2).broadcast_to((128, nb, 16)), op=ALU.mult))
            vinc(lambda v: v.tensor_tensor(out=a_s, in0=a_s, in1=a_s,
                                           op=ALU.mult))
            if nb0 == 0:
                wt("vector", s_g, last_gf[0])
            vinc(lambda v: v.tensor_scalar(out=gA, in0=a_s, scalar1=M2[0],
                                           scalar2=0.0, op0=ALU.is_gt, op1=ALU.add))
            for m2 in M2[1:4]:
                vinc(lambda v, m2=m2: v.scalar_tensor_tensor(
                    out=gA, in0=a_s, scalar=m2, in1=gA, op0=ALU.is_gt, op1=ALU.add))
            vinc(lambda v: v.tensor_scalar(out=gB, in0=a_s, scalar1=M2[6],
                                           scalar2=2.0, op0=ALU.is_gt, op1=ALU.mult))
            vinc(lambda v: v.scalar_tensor_tensor(
                out=gB, in0=a_s, scalar=M2[5], in1=gB, op0=ALU.is_gt, op1=ALU.add))
            vinc(lambda v: v.scalar_tensor_tensor(
                out=gB, in0=a_s, scalar=M2[4], in1=gB, op0=ALU.is_gt, op1=ALU.add))
            cmb = vinc(lambda v: v.scalar_tensor_tensor(
                out=gA, in0=gA, scalar=0.5, in1=gB, op0=ALU.mult, op1=ALU.add))
            wt("scalar", s_v, cmb)
            t2 = sinc(lambda s: s.activation(out=gB, in_=X, func=AF.Sign))
            wt("gpsimd", s_s, t2)
            ginc(lambda g: g.tensor_tensor(
                out=gB.rearrange("p (b k) -> p b k", k=16),
                in0=gB.rearrange("p (b k) -> p b k", k=16),
                in1=bmx.unsqueeze(2).broadcast_to((128, nb, 16)), op=ALU.mult))
            gf = ginc(lambda g: g.tensor_tensor(out=OUT, in0=gA, in1=gB,
                                                op=ALU.mult))
            last_gf[0] = gf
            return {"v": cmb, "s": t2, "g": gf}

        def ln_emit(SRC, DST, SQ, ntt):
            wt("scalar", s_v, E.val(s_v))
            mu = stats[:, 0:ntt]
            q2 = stats[:, 8:8 + ntt]
            var = stats[:, 16:16 + ntt]
            rstd = stats[:, 24:24 + ntt]
            vinc(lambda v: v.tensor_reduce(
                out=mu, in_=SRC.rearrange("p (tt d) -> p tt d", d=D),
                axis=AXX, op=ALU.add))
            t1 = sinc(lambda s: s.activation(out=SQ, in_=SRC, func=AF.Square))
            wt("vector", s_s, t1)
            vinc(lambda v: v.tensor_reduce(
                out=q2, in_=SQ.rearrange("p (tt d) -> p tt d", d=D),
                axis=AXX, op=ALU.add))
            vinc(lambda v: v.tensor_scalar(out=mu, in0=mu, scalar1=1.0 / D,
                                           scalar2=0.0, op0=ALU.mult, op1=ALU.add))
            vinc(lambda v: v.tensor_scalar(out=q2, in0=q2, scalar1=1.0 / D,
                                           scalar2=0.0, op0=ALU.mult, op1=ALU.add))
            vinc(lambda v: v.tensor_tensor(out=var, in0=mu, in1=mu, op=ALU.mult))
            t2 = vinc(lambda v: v.tensor_tensor(out=var, in0=q2, in1=var,
                                                op=ALU.subtract))
            wt("scalar", s_v, t2)
            sinc(lambda s: s.activation(out=rstd, in_=var, func=AF.Ln,
                                        bias=consts[:, 1:2]))
            t3 = sinc(lambda s: s.activation(out=rstd, in_=rstd, func=AF.Exp,
                                             scale=-0.5))
            wt("vector", s_s, t3)
            for tt in range(ntt):
                vinc(lambda v, tt=tt: v.tensor_scalar(
                    out=DST[:, tt * D:(tt + 1) * D],
                    in0=SRC[:, tt * D:(tt + 1) * D],
                    scalar1=mu[:, tt:tt + 1], scalar2=rstd[:, tt:tt + 1],
                    op0=ALU.subtract, op1=ALU.mult))

        ptr_mark = {}
        ptr_ctr = [0]

        def pe_tr(src, dst):
            slot = ptr_ctr[0] % 2
            ptr_ctr[0] += 1
            if slot in ptr_mark:
                wt("tensor", s_s, ptr_mark[slot])
            tn = tinc(lambda t, src=src, slot=slot: t.transpose(
                out=ptr[slot], in_=src, identity=ident))
            wt("scalar", s_t, tn)
            en = sinc(lambda s, dst=dst, slot=slot: s.activation(
                out=dst, in_=ptr[slot], func=AF.Copy))
            ptr_mark[slot] = en
            return en

        n0 = 0
        for dst, src in [(ident, ident_in), (consts, consts_in),
                         (acols, acols_in), (binc, binc_in), (bdtc, bdtc_in),
                         (dcol, dcol_in), (boutr, boutr_in), (gamr, gamr_in),
                         (betr, betr_in), (bbc_f[0:32, 0:1], bbc_in)]:
            n0 = dma("sync", dst, src, sem=s_w1)
        wt("scalar", s_w1, n0)
        wt("vector", s_w1, n0)
        t1 = sinc(lambda s: s.activation(out=acols, in_=acols, func=AF.Exp))
        wt("vector", s_s, t1)
        vinc(lambda v: v.tensor_scalar(out=acols, in0=acols, scalar1=-1.0,
                                       scalar2=0.0, op0=ALU.mult, op1=ALU.add))
        vinc(lambda v: v.tensor_scalar(out=binh, in0=binc, scalar1=0.5,
                                       scalar2=0.0, op0=ALU.mult, op1=ALU.add))

        def cast_weight(src_hbm, dst_scr, ncols):
            for gg in range(GT):
                sl = gg % 2
                nl = dma("sync", wstage[sl][:, 0:ncols],
                         src_hbm[gg * 128:(gg + 1) * 128, :], sem=s_wrow)
                wt("vector", s_wrow, nl)
                cn = vinc(lambda v, sl=sl, ncols=ncols: v.tensor_copy(
                    out=wqstage[sl][:, 0:ncols], in_=wstage[sl][:, 0:ncols]))
                wt("gpsimd", s_v, cn)
                gn = dma("gpsimd", dst_scr[gg * 128:(gg + 1) * 128, :],
                         wqstage[sl][:, 0:ncols])
                wt("sync", s_v, cn)
                wt("vector", s_go, gn)

        cast_weight(w_inT, s_winT, 2 * D)
        cast_weight(w_dtT, s_wdtT, D)
        cast_weight(w_outT, s_woutT, D)
        cast_weight(w_bcT, s_wbcT, 32)

        def prep_ffn(jobs):
            tiles = []
            for (src_hbm, dst_scr, nrows, ncols) in jobs:
                for rt in range(nrows // 128):
                    tiles.append((src_hbm, dst_scr, rt, ncols))
            loads = {}
            qds = {}
            loads[0] = dma("sync", wstage[0][:, 0:tiles[0][3]],
                           tiles[0][0][tiles[0][2] * 128:
                                       (tiles[0][2] + 1) * 128, :], sem=s_wr2[0])
            for i, (src_hbm, dst_scr, rt, ncols) in enumerate(tiles):
                sl = i % 2
                if i + 1 < len(tiles):
                    s2_, d2_, rt2, nc2 = tiles[i + 1]
                    if i - 1 in qds:
                        wt("sync", s_v, qds[i - 1]["v"])
                        wt("sync", s_s, qds[i - 1]["s"])
                        wt("sync", s_g, qds[i - 1]["g"])
                    loads[i + 1] = dma("sync", wstage[(i + 1) % 2][:, 0:nc2],
                                       s2_[rt2 * 128:(rt2 + 1) * 128, :],
                                       sem=s_wr2[(i + 1) % 2])
                wt("vector", s_wr2[sl], loads[i])
                wt("scalar", s_wr2[sl], loads[i])
                mkq = qd_emit(wstage[sl][:, 0:ncols], wqstage[sl][:, 0:ncols],
                              ncols, uprow[:, 0:ncols])
                qds[i] = mkq
                wt("sync", s_g, mkq["g"])
                wt("scalar", s_g, mkq["g"])
                trb = 0
                for cb in range(ncols // 128):
                    q_ = "sync"
                    trb = dma(q_, wq_ev[:, cb * 128:(cb + 1) * 128],
                              wqstage[sl][:, cb * 128:(cb + 1) * 128],
                              sem=s_wtr, transpose=True)
                wt("gpsimd", s_wtr, E.val(s_wtr))
                gn = 0
                for cb in range(ncols // 128):
                    gn = dma("gpsimd", dst_scr[cb * 128:(cb + 1) * 128,
                                               rt * 128:(rt + 1) * 128],
                             wq_ev[:, cb * 128:(cb + 1) * 128])
                wt("vector", s_wtr, E.val(s_wtr))
                wt("sync", s_go, gn)
                wt("scalar", s_go, gn)

        prep_ffn([(w_gate, s_wgT, I, D), (w_up, s_wuT, I, D),
                  (w_down, s_wdnT, D, I)])

        barrier()
        wt("sync", s_go, E.val(s_go))
        wt("gpsimd", s_v, E.val(s_v))

        nl = dma("sync", wout_r.rearrange("p (g d) -> p g d", g=GT),
                 s_woutT.rearrange("(g p) d -> p g d", p=128), sem=s_w1)
        nl = dma("sync", wbc_r.rearrange("p (g d) -> p g d", g=GT),
                 s_wbcT.rearrange("(g p) d -> p g d", p=128), sem=s_w1)
        wt("tensor", s_w1, nl)

        SKIP_P1 = bool(os.environ.get("K_SKIP_P1"))
        SKIP_P2 = bool(os.environ.get("K_SKIP_P2"))
        P1S = int(os.environ.get("K_P1STAGES", "99"))
        x_ch = x_hbm.rearrange("(c tt p) d -> c p tt d", p=128, tt=TT)
        xn_ch = s_xn.rearrange("(c tt p) d -> c p tt d", p=128, tt=TT)
        sg_ch = s_sg.rearrange("(c tt p) d -> c p tt d", p=128, tt=TT)
        ttd = lambda ap: ap.rearrange("p (tt d) -> p tt d", d=D)
        win_pan = s_winT.rearrange("(g p) o -> p g o", p=128)
        wdt_pan = s_wdtT.rearrange("(g p) o -> p g o", p=128)
        scan_marks = {}
        scan_i = [0]
        sc_hc = [0]
        uh_free = {}
        hb_free = {}
        fin_mark = {}
        ug_last = [0]
        xn_sp = [0]

        chunk_gate = [E.val(s_v)]
        for c in range(NCH if not SKIP_P1 else 0):
            par = c % 2
            wt("sync", s_v, chunk_gate[0])
            nx = dma("sync", ttd(xbuf), x_ch[c], sem=s_x)
            pan_cnt = {0: dma("sync", winr[:, 0:GT * 128],
                              win_pan[:, :, 0:128], sem=s_p2[0])}
            wt("vector", s_x, nx)
            wt("scalar", s_x, nx)
            wt("vector", s_go, E.val(s_go))

            ln_emit(xbuf, xn, xss, TT)
            xnv = vinc(lambda v: v.tensor_copy(out=xnb, in_=xn))
            wt("gpsimd", s_v, xnv)
            xn_sp[0] = dma("gpsimd", xn_ch[c], ttd(xn))
            wt("tensor", s_v, xnv)
            for tt in range(TT):
                for gg in range(GT):
                    pe_tr(xnb[:, tt * D + gg * 128: tt * D + (gg + 1) * 128],
                          xnT[:, gg * T + tt * 128: gg * T + (tt + 1) * 128])
            xnT_done = E.val(s_s)

            XZ_NJ = int(os.environ.get("K_XZ_NJ", "16"))
            XZ_MM = os.environ.get("K_XZ_MM", "1") == "1"
            XZ_EV = os.environ.get("K_XZ_EV", "1") == "1"
            last_mt = {}
            wt("tensor", s_s, xnT_done)
            for j in range((XZ_NJ if P1S >= 2 else 0)):
                pm = p_mm[j % 2]
                if j + 1 < 16:
                    if j - 1 in last_mt:
                        wt("sync", s_t, last_mt[j - 1])
                    pan_cnt[j + 1] = dma(
                        "sync", winr[:, ((j + 1) % 2) * GT * 128:
                                     (((j + 1) % 2) + 1) * GT * 128],
                        win_pan[:, :, (j + 1) * 128:(j + 2) * 128],
                        sem=s_p2[(j + 1) % 2])
                wt("tensor", s_p2[j % 2], pan_cnt[j])
                for gg in range(GT if XZ_MM else 0):
                    def mm(t, j=j, gg=gg, pm=pm):
                        return t.matmul(
                            out=pm[:, 0:T],
                            lhsT=winr[:, (j % 2) * GT * 128 + gg * 128:
                                      (j % 2) * GT * 128 + (gg + 1) * 128],
                            rhs=xnT[:, gg * T:(gg + 1) * T],
                            start=(gg == 0), stop=(gg == GT - 1))
                    if gg == GT - 1:
                        mt = tinc(mm)
                    else:
                        t_(lambda t, f=mm: f(t))
                if not XZ_MM:
                    mt = E.inc(s_t, 1)
                    E.p["tensor"].append(lambda t: t.nop().then_inc(s_t, 1))
                last_mt[j] = mt
                wt("vector", s_t, mt)
                if not XZ_EV:
                    ev = vinc(lambda v: v.nop())
                    wt("tensor", s_v, ev)
                    continue
                if j < 8:
                    ev = vinc(lambda v, j=j, pm=pm: v.tensor_scalar(
                        out=xss[:, j * T:(j + 1) * T], in0=pm[:, 0:T],
                        scalar1=binc[:, j:j + 1], scalar2=0.0,
                        op0=ALU.add, op1=ALU.add))
                else:
                    jj = j - 8
                    ZL = int(os.environ.get("K_ZL", "3"))
                    ZNS = bool(os.environ.get("K_Z_NOSCAL"))
                    ev0 = vinc(lambda v, j=j, jj=jj, pm=pm: v.tensor_scalar(
                        out=zs[:, jj * T:(jj + 1) * T], in0=pm[:, 0:T],
                        scalar1=binc[:, j:j + 1], scalar2=0.0,
                        op0=ALU.add, op1=ALU.add))
                    ev = ev0
                    if ZL >= 2:
                        if ZNS:
                            ev = vinc(lambda v, jj=jj: v.tensor_copy(
                                out=ln_scr[:, 0:T],
                                in_=zs[:, jj * T:(jj + 1) * T]))
                        else:
                            wt("scalar", s_v, ev0)
                            th = sinc(lambda s, jj=jj: s.activation(
                                out=ln_scr[:, 0:T],
                                in_=zs[:, jj * T:(jj + 1) * T],
                                func=AF.Sigmoid))
                            wt("vector", s_s, th)
                    if ZL >= 3:
                        ev = vinc(lambda v, jj=jj: v.tensor_tensor(
                            out=zs[:, jj * T:(jj + 1) * T],
                            in0=zs[:, jj * T:(jj + 1) * T],
                            in1=ln_scr[:, 0:T], op=ALU.mult))
                    wt("scalar", s_v, ev)
                wt("tensor", s_v, ev)

            xssb = xnT
            if P1S >= 3:
                wt("vector", s_t, last_mt[15])
                csb = vinc(lambda v: v.tensor_copy(out=xssb, in_=xss))
                wt("tensor", s_v, csb)

            if P1S >= 3:
                wt("sync", s_t, last_mt[14])
                dt_cnt = {0: dma("sync", wdtr[:, 0:GT * 128],
                                 wdt_pan[:, :, 0:128], sem=s_p2[0])}
            for j in range(8 if P1S >= 3 else 0):
                pm = p_mm[j % 2]
                if j + 1 < 8:
                    if j - 1 in last_mt:
                        wt("sync", s_t, last_mt[j - 1])
                    else:
                        wt("sync", s_t, last_mt[15])
                    dt_cnt[j + 1] = dma(
                        "sync", wdtr[:, ((j + 1) % 2) * GT * 128:
                                     (((j + 1) % 2) + 1) * GT * 128],
                        wdt_pan[:, :, (j + 1) * 128:(j + 2) * 128],
                        sem=s_p2[(j + 1) % 2])
                wt("tensor", s_p2[j % 2], dt_cnt[j])
                for gg in range(GT):
                    def mm(t, j=j, gg=gg, pm=pm):
                        return t.matmul(
                            out=pm[:, 0:T],
                            lhsT=wdtr[:, (j % 2) * GT * 128 + gg * 128:
                                      (j % 2) * GT * 128 + (gg + 1) * 128],
                            rhs=xssb[:, gg * T:(gg + 1) * T],
                            start=(gg == 0), stop=(gg == GT - 1))
                    if gg == GT - 1:
                        mt = tinc(mm)
                    else:
                        t_(lambda t, f=mm: f(t))
                last_mt[j] = mt
                wt("vector", s_t, mt)
                xb = vinc(lambda v, j=j, pm=pm: v.tensor_scalar(
                    out=ln_scr[:, T:2 * T], in0=pm[:, 0:T],
                    scalar1=bdtc[:, j:j + 1], scalar2=0.0,
                    op0=ALU.add, op1=ALU.add))
                wt("scalar", s_v, xb)
                sinc(lambda s: s.activation(out=ln_scr[:, 0:T],
                                            in_=ln_scr[:, T:2 * T],
                                            func=AF.Abs))
                sinc(lambda s: s.activation(out=ln_scr[:, 0:T],
                                            in_=ln_scr[:, 0:T],
                                            func=AF.Exp, scale=-1.0))
                sinc(lambda s: s.activation(out=ln_scr[:, 0:T],
                                            in_=ln_scr[:, 0:T],
                                            func=AF.Ln, bias=consts[:, 0:1]))
                rl = sinc(lambda s: s.activation(
                    out=ln_scr[:, T:2 * T], in_=ln_scr[:, T:2 * T],
                    func=AF.Relu))
                wt("vector", s_s, rl)
                ev = vinc(lambda v, j=j: v.tensor_tensor(
                    out=dts[:, j * T:(j + 1) * T], in0=ln_scr[:, 0:T],
                    in1=ln_scr[:, T:2 * T], op=ALU.add))
                wt("tensor", s_v, ev)
                wt("scalar", s_v, ev)

            for gg in range(GT if P1S >= 4 else 0):
                def mm(t, gg=gg):
                    return t.matmul(
                        out=PSF[0:32, 0:T],
                        lhsT=wbc_r[:, gg * 32:(gg + 1) * 32],
                        rhs=xssb[:, gg * T:(gg + 1) * T],
                        start=(gg == 0), stop=(gg == GT - 1))
                if gg == GT - 1:
                    mt = tinc(mm)
                else:
                    t_(lambda t, f=mm: f(t))
            if P1S >= 4:
                wt("vector", s_t, mt)
                bcv = vinc(lambda v: v.tensor_scalar(
                    out=bcbf[0:32, :], in0=PSF[0:32, 0:T],
                    scalar1=bbc_f[0:32, 0:1], scalar2=0.0,
                    op0=ALU.add, op1=ALU.add))
                wt("tensor", s_v, bcv)
                wt("gpsimd", s_v, bcv)
                b1 = dma("gpsimd", s_bc[par], bcbf[0:32, :])
                wt("sync", s_go, b1)
                dma("sync", brep,
                    s_bc[par, 0:16].rearrange("n t -> (n t)").partition_broadcast(128),
                    sem=s_x)
                b3 = dma("sync", crep,
                         s_bc[par, 16:32].rearrange("n t -> (n t)").partition_broadcast(128),
                         sem=s_x)
                wt("vector", s_x, b3)
                bc_mt = mt

            wt("vector", s_go, E.val(s_go))
            for gg in range(GT if P1S >= 5 else 0):
                vinc(lambda v, gg=gg: v.tensor_tensor(
                    out=dtxb, in0=dts[:, gg * T:(gg + 1) * T],
                    in1=xss[:, gg * T:(gg + 1) * T], op=ALU.mult))
                for half in range(2):
                    vinc(lambda v, half=half: v.tensor_tensor(
                        out=uhalf.rearrange("p (n t) -> p n t", n=8),
                        in0=dtxb.unsqueeze(1).broadcast_to((128, 8, T)),
                        in1=brep[:, half * 8 * T:(half * 8 + 8) * T]
                        .rearrange("p (n t) -> p n t", n=8),
                        op=ALU.mult))
                    for k in range(8):
                        n_ = half * 8 + k
                        i_ = scan_i[0]
                        slot = i_ % 4
                        if i_ >= 4:
                            wt("scalar", s_v, scan_marks[i_ - 4])
                        da = sinc(lambda s, gg=gg, n_=n_, slot=slot:
                                  s.activation(
                                      out=dA[:, slot * T:(slot + 1) * T],
                                      in_=dts[:, gg * T:(gg + 1) * T],
                                      func=AF.Exp,
                                      scale=acols[:, gg * 16 + n_:
                                                  gg * 16 + n_ + 1]))
                        wt("vector", s_s, da)
                        if c == 0:
                            init = 0.0
                        else:
                            init = carry[:, (1 - par) * 128 + gg * 16 + n_:
                                         (1 - par) * 128 + gg * 16 + n_ + 1]
                        mk = vinc(lambda v, k=k, slot=slot, init=init:
                                  v.tensor_tensor_scan(
                                      out=hbuf[:, k * T:(k + 1) * T],
                                      data0=dA[:, slot * T:(slot + 1) * T],
                                      data1=uhalf[:, k * T:(k + 1) * T],
                                      initial=init, op0=ALU.mult, op1=ALU.add))
                        scan_marks[i_] = mk
                        scan_i[0] += 1
                    vinc(lambda v, gg=gg, half=half, par=par: v.tensor_copy(
                        out=carry[:, par * 128 + gg * 16 + half * 8:
                                  par * 128 + gg * 16 + half * 8 + 8],
                        in_=hbuf[:, T - 1::T]))
                    vinc(lambda v, half=half: v.tensor_tensor(
                        out=hbuf, in0=hbuf,
                        in1=crep[:, half * 8 * T:(half * 8 + 8) * T],
                        op=ALU.mult))
                    w = 8
                    while w > 1:
                        w //= 2
                        vinc(lambda v, w=w: v.tensor_tensor(
                            out=hbuf[:, 0:w * T], in0=hbuf[:, 0:w * T],
                            in1=hbuf[:, w * T:2 * w * T], op=ALU.add))
                    vinc(lambda v, half=half: v.tensor_copy(
                        out=ln_scr[:, half * T:(half + 1) * T],
                        in_=hbuf[:, 0:T]))
                vinc(lambda v: v.tensor_tensor(
                    out=ln_scr[:, 0:T], in0=ln_scr[:, 0:T],
                    in1=ln_scr[:, T:2 * T], op=ALU.add))
                vinc(lambda v, gg=gg: v.scalar_tensor_tensor(
                    out=ln_scr[:, 0:T], in0=xss[:, gg * T:(gg + 1) * T],
                    scalar=dcol[:, gg:gg + 1], in1=ln_scr[:, 0:T],
                    op0=ALU.mult, op1=ALU.add))
                vinc(lambda v, gg=gg: v.tensor_tensor(
                    out=y2b[:, gg * T:(gg + 1) * T], in0=ln_scr[:, 0:T],
                    in1=zs[:, gg * T:(gg + 1) * T], op=ALU.mult))
            barrier()

            if P1S >= 6:
                wt("vector", s_go, xn_sp[0])
                vinc(lambda v: v.tensor_tensor(
                    out=ttd(xn), in0=ttd(xn),
                    in1=boutr.unsqueeze(1).broadcast_to((128, TT, D)),
                    op=ALU.add))
            for tt in range(TT if P1S >= 6 else 0):
                for dh in range(2):
                    pm = p_mm[dh]
                    for gg in range(GT):
                        def mm(t, tt=tt, gg=gg, dh=dh, pm=pm):
                            return t.matmul(
                                out=pm[:, 0:512],
                                lhsT=y2b[:, gg * T + tt * 128:
                                         gg * T + (tt + 1) * 128],
                                rhs=wout_r[:, gg * D + dh * 512:
                                           gg * D + (dh + 1) * 512],
                                start=(gg == 0), stop=(gg == GT - 1))
                        if gg == GT - 1:
                            mt = tinc(mm)
                        else:
                            t_(lambda t, f=mm: f(t))
                    wt("vector", s_t, mt)
                    ev = vinc(lambda v, tt=tt, dh=dh, pm=pm:
                              v.scalar_tensor_tensor(
                        out=pre[:, tt * D + dh * 512: tt * D + (dh + 1) * 512],
                        in0=pm[:, 0:512], scalar=0.0,
                        in1=xn[:, tt * D + dh * 512: tt * D + (dh + 1) * 512],
                        op0=ALU.add, op1=ALU.add))
                    wt("tensor", s_v, ev)

            if P1S >= 7:
                ln_emit(pre, pre, xss, TT)
            for tt in range(TT if P1S >= 7 else 0):
                vinc(lambda v, tt=tt: v.tensor_tensor(
                    out=pre[:, tt * D:(tt + 1) * D],
                    in0=pre[:, tt * D:(tt + 1) * D], in1=gamr, op=ALU.mult))
                vinc(lambda v, tt=tt: v.tensor_tensor(
                    out=pre[:, tt * D:(tt + 1) * D],
                    in0=pre[:, tt * D:(tt + 1) * D], in1=betr, op=ALU.add))
            if P1S >= 7:
                sv = vinc(lambda v: v.tensor_copy(out=xnb, in_=pre))
                wt("gpsimd", s_v, sv)
                dma("gpsimd", sg_ch[c], ttd(xnb))
            barrier()
            chunk_gate[0] = E.val(s_v)

        wt("sync", s_go, E.val(s_go))

        wt("sync", s_v, chunk_gate[0])
        ng = dma("sync", wg_r.rearrange("p (g i) -> p g i", g=GT),
                 s_wgT.rearrange("(g p) i -> p g i", p=128), sem=s_w1)
        wt("tensor", s_w1, ng)
        wu_pan = s_wuT.rearrange("(g p) i -> p g i", p=128)
        x_tl = x_hbm.rearrange("(q p) d -> q p d", p=128)
        xn_tl = s_xn.rearrange("(q p) d -> q p d", p=128)
        sg_tl = s_sg.rearrange("(q p) d -> q p d", p=128)
        out_tl = out_hbm.rearrange("(q p) d -> q p d", p=128)
        up_mark = {}
        up_ld = {}
        dn_mark = {}
        pgu_free = {}
        pgu_ctr = [0]
        dn_evac_prev = [0]
        tr_last = [0]
        tl_ld = {}
        tl_free = {}

        def up_load(gsec):
            sec = gsec % 6
            return dma("scalar",
                       wur[:, (gsec % 2) * GT * 512:
                           (gsec % 2) * GT * 512 + GT * SECW[sec]],
                       wu_pan[:, :, SECO[sec]:SECO[sec] + SECW[sec]],
                       sem=s_p2[gsec % 2])

        def p2_loads(qq, par2):
            sm = s_tl2[par2]
            dma("gpsimd", xn2[:, par2 * 2 * D:par2 * 2 * D + D],
                xn_tl[2 * qq], sem=sm)
            dma("gpsimd", xn2[:, par2 * 2 * D + D:(par2 + 1) * 2 * D],
                xn_tl[2 * qq + 1], sem=sm)
            dma("gpsimd", s2[:, par2 * 2 * D:par2 * 2 * D + D],
                sg_tl[2 * qq], sem=sm)
            return dma("gpsimd", s2[:, par2 * 2 * D + D:(par2 + 1) * 2 * D],
                       sg_tl[2 * qq + 1], sem=sm)

        if NPAIR and not SKIP_P2:
            tl_ld[0] = p2_loads(0, 0)
        pair_mk = {}
        xtr_last = [0]
        dn_v_last = [0]

        def pre_emit(qq):
            par2 = qq % 2
            wt("vector", s_tl2[par2], tl_ld[qq])
            wt("scalar", s_tl2[par2], tl_ld[qq])
            wt("gpsimd", s_t, xtr_last[0])
            pair_mk[qq] = qd_emit(xn2[:, par2 * 2 * D:(par2 + 1) * 2 * D],
                                  xnq, 2 * D, uprow[:, 0:2 * D])

        if NPAIR and not SKIP_P2:
            pre_emit(0)
        for q in range(NPAIR if not SKIP_P2 else 0):
            q0, q1 = 2 * q, 2 * q + 1
            par2 = q % 2
            s2v = s2[:, par2 * 2 * D:(par2 + 1) * 2 * D]
            mk = pair_mk[q]
            if q + 1 < NPAIR:
                if q - 1 in tl_free:
                    wt("gpsimd", s_v, tl_free[q - 1][0])
                    wt("gpsimd", s_s, tl_free[q - 1][1])
                tl_ld[q + 1] = p2_loads(q + 1, (q + 1) % 2)
            wt("sync", s_v, tl_free.get(q - 1, (0, 0))[0])
            nx2 = dma("sync", x2[:, 0:D], x_tl[q0], sem=s_x)
            nx2 = dma("sync", x2[:, D:2 * D], x_tl[q1], sem=s_x)
            for pre_s in (0, 1):
                gsec = q * 6 + pre_s
                if gsec - 2 in up_mark:
                    wt("scalar", s_t, up_mark[gsec - 2])
                up_ld[gsec] = up_load(gsec)
            wt("scalar", s_go, E.val(s_go))

            wt("tensor", s_g, mk["g"])
            for ti in range(2):
                for gg in range(GT):
                    pe_tr(xnq[:, ti * D + gg * 128: ti * D + (gg + 1) * 128],
                          xnqT[:, gg * 256 + ti * 128: gg * 256 + (ti + 1) * 128])
            xtr_last[0] = E.val(s_t)
            xnqT_done = E.val(s_s)

            for sec in range(6):
                w_ = SECW[sec]
                gsec = q * 6 + sec
                uslot = gsec % 2
                wt("tensor", s_p2[uslot], up_ld[gsec])
                if sec == 0:
                    wt("tensor", s_s, xnqT_done)
                    wt("scalar", s_v, dn_v_last[0])
                for ti in range(2):
                    for which in range(2):
                        pslot = pgu_ctr[0] % 2
                        ps = pgu[pslot]
                        pgu_ctr[0] += 1
                        if pslot in pgu_free:
                            wt("tensor", s_s, pgu_free[pslot])
                        for gg in range(GT):
                            if which == 0:
                                def mmf(t, ti=ti, sec=sec, gg=gg, w_=w_, ps=ps):
                                    return t.matmul(
                                        out=ps[:, 0:w_],
                                        lhsT=xnqT[:, gg * 256 + ti * 128:
                                                  gg * 256 + (ti + 1) * 128],
                                        rhs=wg_r[:, gg * I + SECO[sec]:
                                                 gg * I + SECO[sec] + w_],
                                        start=(gg == 0), stop=(gg == GT - 1))
                            else:
                                def mmf(t, ti=ti, gg=gg, w_=w_, uslot=uslot,
                                        ps=ps):
                                    return t.matmul(
                                        out=ps[:, 0:w_],
                                        lhsT=xnqT[:, gg * 256 + ti * 128:
                                                  gg * 256 + (ti + 1) * 128],
                                        rhs=wur[:, uslot * GT * 512 + gg * w_:
                                                uslot * GT * 512 + (gg + 1) * w_],
                                        start=(gg == 0), stop=(gg == GT - 1))
                            if gg == GT - 1:
                                mt = tinc(mmf)
                            else:
                                t_(lambda t, f=mmf: f(t))
                        wt("scalar", s_t, mt)
                        dstb = hstage if which == 0 else uprow
                        pgu_free[pslot] = sinc(
                            lambda s, ps=ps, dstb=dstb, ti=ti, sec=sec, w_=w_:
                            s.activation(out=dstb[:, ti * I + SECO[sec]:
                                                  ti * I + SECO[sec] + w_],
                                         in_=ps[:, 0:w_], func=AF.Copy))
                        if which == 1:
                            up_mark[gsec] = mt
                if sec + 2 < 6:
                    wt("scalar", s_t, up_mark[gsec])
                    up_ld[gsec + 2] = up_load(gsec + 2)

            SLC = [(0, 768), (768, 768), (1536, 768), (2304, 512)]
            dn_ready = {}
            prev_g = mk["g"]
            qd_h1 = None
            wt("gpsimd", s_t, tr_last[0])
            for ti in range(2):
                hsv = hstage[:, ti * I:(ti + 1) * I]
                upv = uprow[:, ti * I:(ti + 1) * I]
                hqv = hq[:, ti * I:(ti + 1) * I]
                parh = qd_ctr[0] % 2
                qd_ctr[0] += 1
                wt("scalar", s_g, prev_g)
                sigm = {}
                for (c0, cw) in SLC:
                    sigm[c0] = sinc(lambda s, hsv=hsv, c0=c0, cw=cw:
                                    s.activation(
                        out=qpool[:, c0:c0 + cw], in_=hsv[:, c0:c0 + cw],
                        func=AF.Sigmoid))
                for (c0, cw) in SLC:
                    wt("vector", s_s, sigm[c0])
                    vinc(lambda v, hsv=hsv, c0=c0, cw=cw: v.tensor_tensor(
                        out=hsv[:, c0:c0 + cw], in0=hsv[:, c0:c0 + cw],
                        in1=qpool[:, c0:c0 + cw], op=ALU.mult))
                    vinc(lambda v, hsv=hsv, upv=upv, c0=c0, cw=cw:
                         v.tensor_tensor(
                             out=hsv[:, c0:c0 + cw], in0=hsv[:, c0:c0 + cw],
                             in1=upv[:, c0:c0 + cw], op=ALU.mult))
                    mkh = qd_emit(hsv[:, c0:c0 + cw], hqv[:, c0:c0 + cw],
                                  cw, upv[:, c0:c0 + cw],
                                  par=parh, nb0=c0 // 16)
                    wt("tensor", s_g, mkh["g"])
                    for ib in range(c0 // 128, (c0 + cw) // 128):
                        en = pe_tr(hqv[:, ib * 128:(ib + 1) * 128],
                                   hqT[:, ib * 256 + ti * 128:
                                       ib * 256 + (ti + 1) * 128])
                        if ti == 1:
                            dn_ready[ib] = en
                prev_g = mkh["g"]
                qd_h1 = mkh
            tr_last[0] = E.val(s_t)

            wt("tensor", s_s, dn_evac_prev[0])
            for ib in range(NIB):
                slot = ib % 2
                gib = q * NIB + ib
                if gib - 2 in dn_mark:
                    wt("sync", s_t, dn_mark[gib - 2])
                nd = dma("sync", dnr[:, slot * D:(slot + 1) * D],
                         s_wdnT[ib * 128:(ib + 1) * 128, :], sem=s_d4[slot])
                wt("tensor", s_d4[slot], nd)
                wt("tensor", s_s, dn_ready[ib])
                for ti in range(2):
                    for dh in range(2):
                        def mmd(t, ib=ib, ti=ti, dh=dh, slot=slot):
                            return t.matmul(
                                out=p_dn[ti * 2 + dh][:, 0:512],
                                lhsT=hqT[:, ib * 256 + ti * 128:
                                         ib * 256 + (ti + 1) * 128],
                                rhs=dnr[:, slot * D + dh * 512:
                                        slot * D + (dh + 1) * 512],
                                start=(ib == 0), stop=(ib == NIB - 1))
                        if ti == 1 and dh == 1:
                            mt = tinc(mmd)
                        else:
                            t_(lambda t, f=mmd: f(t))
                dn_mark[gib] = mt
            if q + 1 < NPAIR:
                pre_emit(q + 1)
            wt("scalar", s_t, mt)
            wt("scalar", s_g, qd_h1["g"])
            ev_marks = []
            it = 0
            for ti in range(2):
                wt("vector", s_go, E.val(s_go))
                for dh in range(2):
                    pm = p_dn[ti * 2 + dh]
                    p_ = it % 2
                    g0 = gsc[:, p_ * 1024:p_ * 1024 + 512]
                    g1 = gsc[:, p_ * 1024 + 512:(p_ + 1) * 1024]
                    if it >= 2:
                        wt("scalar", s_v, ev_marks[it - 2])
                    sinc(lambda s, pm=pm, g0=g0: s.activation(
                        out=g0, in_=pm[:, 0:512], func=AF.Copy))
                    th = sinc(lambda s, g0=g0, g1=g1: s.activation(
                        out=g1, in_=g0, func=AF.Sigmoid))
                    wt("vector", s_s, th)
                    vinc(lambda v, g0=g0, g1=g1: v.tensor_tensor(
                        out=g0, in0=g0, in1=g1, op=ALU.mult))
                    o0 = ti * D + dh * 512
                    wt("vector", s_x, nx2)
                    vinc(lambda v, o0=o0, dh=dh, s2v=s2v, g0=g0:
                         v.tensor_tensor(
                        out=obuf[:, dh * 512:(dh + 1) * 512], in0=g0,
                        in1=s2v[:, o0:o0 + 512], op=ALU.mult))
                    ev = vinc(lambda v, o0=o0, dh=dh: v.tensor_tensor(
                        out=obuf[:, dh * 512:(dh + 1) * 512],
                        in0=obuf[:, dh * 512:(dh + 1) * 512],
                        in1=x2[:, o0:o0 + 512], op=ALU.add))
                    ev_marks.append(ev)
                    it += 1
                ov = E.val(s_v)
                wt("gpsimd", s_v, ov)
                dma("gpsimd", out_tl[q0 + ti], obuf)
            dn_evac_prev[0] = E.val(s_s)
            dn_v_last[0] = ov
            tl_free[q] = (ov, mk["s"])

        E.p["gpsimd"].append(lambda g: g.wait_ge(s_go, E.val(s_go)))

        block = es.enter_context(nc.Block())

        @block.sync
        def _(sy):
            for fn in E.p["sync"]:
                fn(sy)

        @block.vector
        def _(v):
            for fn in E.p["vector"]:
                fn(v)

        @block.scalar
        def _(s):
            for fn in E.p["scalar"]:
                fn(s)

        @block.gpsimd
        def _(g):
            for fn in E.p["gpsimd"]:
                fn(g)

        @block.tensor
        def _(t):
            for fn in E.p["tensor"]:
                fn(t)

    return nc


def _host_layouts(inputs):
    f32 = np.float32
    g = lambda k: np.asarray(inputs[k], f32)
    w_in = g("L0_attn_w_in"); b_in = g("L0_attn_b_in")
    w_dt = g("L0_attn_w_dt"); b_dt = g("L0_attn_b_dt")
    A_log = g("L0_attn_A_log")
    w_B = g("L0_attn_w_B"); b_B = g("L0_attn_b_B")
    w_C = g("L0_attn_w_C"); b_C = g("L0_attn_b_C")
    Dv = g("L0_attn_D")
    w_out = g("L0_attn_w_out"); b_out = g("L0_attn_b_out")
    gamma = g("L0_attn_gamma"); beta = g("L0_attn_beta")
    w_gate = g("L0_ffn_w_gate"); w_up = g("L0_ffn_w_up")
    w_dn = g("L0_ffn_w_down")

    bf = ml_dtypes.bfloat16
    consts = np.zeros((128, 8), f32)
    consts[:, 0] = 1.0
    consts[:, 1] = LN_EPS
    return {
        "w_inT": np.ascontiguousarray(w_in.T),
        "w_dtT": np.ascontiguousarray(w_dt.T),
        "w_outT": np.ascontiguousarray(w_out.T),
        "w_bcT": np.ascontiguousarray(np.concatenate([w_B.T, w_C.T], axis=1)),
        "w_gate": np.ascontiguousarray(w_gate),
        "w_up": np.ascontiguousarray(w_up),
        "w_down": np.ascontiguousarray(w_dn),
        "a_cols_log": np.ascontiguousarray(
            A_log.reshape(8, 128, NST).transpose(1, 0, 2).reshape(128, 128)),
        "b_in_cols": np.ascontiguousarray(b_in.reshape(16, 128).T),
        "b_dt_cols": np.ascontiguousarray(b_dt.reshape(8, 128).T),
        "bbc_col": np.ascontiguousarray(
            np.concatenate([b_B, b_C]).reshape(32, 1)),
        "d_cols": np.ascontiguousarray(Dv.reshape(8, 128).T),
        "b_out_rep": np.ascontiguousarray(
            np.broadcast_to(b_out, (128, D))).astype(bf),
        "gamma_rep": np.ascontiguousarray(
            np.broadcast_to(gamma, (128, D))).astype(bf),
        "beta_rep": np.ascontiguousarray(
            np.broadcast_to(beta, (128, D))).astype(bf),
        "ident_bf": np.eye(128, dtype=bf),
        "consts": consts,
    }


def kernel(**inputs):
    x = np.asarray(inputs["L0_attn_x"], np.float32)
    B, L, _ = x.shape
    shared = _host_layouts(inputs)
    nc = build_program(L=L, T=512)
    in_maps = []
    for b in range(B):
        m = dict(shared)
        m["x_b"] = np.ascontiguousarray(x[b])
        in_maps.append(m)
    res = run_bass_kernel_spmd(nc, in_maps, core_ids=list(range(B)))
    return np.stack([np.asarray(res.results[b]["out_b"]) for b in range(B)],
                    axis=0)


# version 11 (final)
# speedup vs baseline: 1.2843x; 1.0478x over previous
import contextlib
import os
import numpy as np
import ml_dtypes
import concourse.bass as bass
import concourse.mybir as mybir
from concourse.bass_utils import run_bass_kernel_spmd

F32 = mybir.dt.float32
BF16 = mybir.dt.bfloat16
AF = mybir.ActivationFunctionType
ALU = mybir.AluOpType
AXX = mybir.AxisListType.X

D = 1024
NST = 16
I = 2816
GT = 8
NIB = I // 128
SECW = [512, 512, 512, 512, 512, 256]
SECO = [0, 512, 1024, 1536, 2048, 2560]
LN_EPS = 1e-5
M2 = [0.0625, 0.5625, 1.5625, 3.0625, 6.25, 12.25, 25.0]
M1 = [0.25, 0.75, 1.25, 1.75, 2.5, 3.5, 5.0]

PF_COLS = 24192
PB_COLS = 53504


class Plan:
    def __init__(self):
        self.p = {e: [] for e in ("sync", "vector", "scalar", "gpsimd", "tensor")}
        self.tot = {}

    def inc(self, sem, k):
        key = id(sem)
        self.tot[key] = self.tot.get(key, 0) + k
        return self.tot[key]

    def val(self, sem):
        return self.tot.get(id(sem), 0)


def build_program(L=8192, T=512):
    NCH = L // T
    TT = T // 128
    NTIL = L // 128
    NPAIR = NTIL // 2

    nc = bass.Bass("TRN2", target_bir_lowering=False)

    def din(name, shape, dtype=F32):
        return nc.dram_tensor(name, shape, dtype, kind="ExternalInput").ap()

    x_hbm = din("x_b", [L, D])
    w_inT = din("w_inT", [D, 2 * D])
    w_dtT = din("w_dtT", [D, D])
    w_outT = din("w_outT", [D, D])
    w_bcT = din("w_bcT", [D, 32])
    w_gate = din("w_gate", [I, D])
    w_up = din("w_up", [I, D])
    w_down = din("w_down", [D, I])
    acols_in = din("a_cols_log", [128, 128])
    binc_in = din("b_in_cols", [128, 16])
    bdtc_in = din("b_dt_cols", [128, 8])
    bbc_in = din("bbc_col", [32, 1])
    dcol_in = din("d_cols", [128, 8])
    boutr_in = din("b_out_rep", [128, D], BF16)
    gamr_in = din("gamma_rep", [128, D], BF16)
    betr_in = din("beta_rep", [128, D], BF16)
    ident_in = din("ident_bf", [128, 128], BF16)
    consts_in = din("consts", [128, 8])

    out_hbm = nc.dram_tensor("out_b", [L, D], F32, kind="ExternalOutput").ap()

    def scr(name, shape, dtype):
        return nc.dram_tensor(name, shape, dtype, kind="Internal").ap()

    s_winT = scr("s_winT", [D, 2 * D], BF16)
    s_wdtT = scr("s_wdtT", [D, D], BF16)
    s_woutT = scr("s_woutT", [D, D], BF16)
    s_wbcT = scr("s_wbcT", [D, 32], BF16)
    s_wgT = scr("s_wgT", [D, I], BF16)
    s_wuT = scr("s_wuT", [D, I], BF16)
    s_wdnT = scr("s_wdnT", [I, D], BF16)
    s_xn = scr("s_xn", [L, D], F32)
    s_sg = scr("s_sg", [L, D], BF16)
    s_bc = scr("s_bc", [2, 32, T], BF16)

    E = Plan()
    es = contextlib.ExitStack()
    with es:
        PF = es.enter_context(nc.sbuf_tensor("PF", [128, PF_COLS], F32))
        PB = es.enter_context(nc.sbuf_tensor("PB", [128, PB_COLS], BF16))

        class Alloc:
            def __init__(self, pool, limit):
                self.pool, self.limit, self.off = pool, limit, 0

            def take(self, n):
                a = self.off
                self.off += n
                assert self.off <= self.limit, f"pool overflow {self.off}>{self.limit}"
                return self.pool[:, a:a + n]

        af = Alloc(PF, PF_COLS)
        ab = Alloc(PB, PB_COLS)

        consts = af.take(8)
        acols = af.take(128)
        binc = af.take(16)
        binh = af.take(16)
        bdtc = af.take(8)
        dcol = af.take(8)
        stats = af.take(64)
        bbc_f = af.take(32)
        ident = ab.take(128)
        boutr = ab.take(D)

        f_base = af.off
        b_base = ab.off

        xbuf = af.take(TT * D)
        pre = xbuf
        xn = af.take(TT * D)
        xss = af.take(GT * T)
        zs = af.take(GT * T)
        dts = af.take(GT * T)
        dA = af.take(4 * T)
        carry = af.take(2 * 128)
        ln_scr = af.take(2 * T)
        p1_f_end = af.off

        xnb = ab.take(TT * D)
        xnT = ab.take(GT * T)
        y2b = ab.take(GT * T)
        brep = ab.take(NST * T)
        crep = ab.take(NST * T)
        uhalf = ab.take(8 * T)
        hbuf = ab.take(8 * T)
        dtxb = ab.take(T)
        winr = ab.take(2 * GT * 128)
        wdtr = ab.take(2 * GT * 128)
        wout_r = ab.take(GT * D)
        wbc_r = ab.take(GT * 32)
        gamr = ab.take(D)
        betr = ab.take(D)
        p1_b_end = ab.off
        bcbf = hbuf[:, 0:T]

        af.off = f_base
        ab.off = b_base
        xn2 = af.take(4 * D)
        x2 = af.take(2 * D)
        qpool = af.take(2 * I)
        qstat = af.take(768)
        hstage = af.take(2 * I)
        uprow = af.take(2 * I)
        gsc = hstage[:, 0:2048]
        obuf = hstage[:, I:I + D]
        assert af.off <= p1_f_end, af.off

        wg_r = ab.take(GT * I)
        wur = ab.take(2 * GT * 512)
        dnr = ab.take(2 * D)
        xnq = ab.take(2 * D)
        xnqT = ab.take(2 * GT * 128)
        hq = ab.take(2 * I)
        hqT = ab.take(2 * NIB * 128)
        s2 = ab.take(4 * D)
        assert ab.off <= max(p1_b_end, ab.off)
        af.off = max(af.off, p1_f_end)
        ab.off = max(ab.off, p1_b_end)

        wstage = [hstage[:, 0:I], hstage[:, I:2 * I]]
        wqstage = [wg_r[:, 0:I], wg_r[:, I:2 * I]]
        wq_ev = wg_r[:, 2 * I:4 * I]

        PSF = es.enter_context(nc.psum_tensor("PSF", [128, 3072], F32))
        PSB = es.enter_context(nc.psum_tensor("PSB", [128, 2048], BF16))
        p_dn = [PSF[:, k * 512:(k + 1) * 512] for k in range(4)]
        pgu = [PSF[:, 2048:2560], PSF[:, 2560:3072]]
        p_mm = [pgu[0], pgu[1]]
        ptr = [PSB[:, 0:128], PSB[:, 1024:1152]]

        s_ld = es.enter_context(nc.semaphore("s_ld"))
        s_w1 = es.enter_context(nc.semaphore("s_w1"))
        s_wrow = es.enter_context(nc.semaphore("s_wrow"))
        s_x = es.enter_context(nc.semaphore("s_x"))
        s_p2 = [es.enter_context(nc.semaphore(f"s_p2_{i}")) for i in range(2)]
        s_wr2 = [es.enter_context(nc.semaphore(f"s_wr2_{i}")) for i in range(2)]
        s_wtr = es.enter_context(nc.semaphore("s_wtr"))
        s_tl2 = [es.enter_context(nc.semaphore(f"s_tl2_{i}")) for i in range(2)]
        s_d4 = [es.enter_context(nc.semaphore(f"s_d4_{i}")) for i in range(4)]
        s_go = es.enter_context(nc.semaphore("s_go"))
        s_v = es.enter_context(nc.semaphore("s_v"))
        s_s = es.enter_context(nc.semaphore("s_s"))
        s_t = es.enter_context(nc.semaphore("s_t"))
        s_g = es.enter_context(nc.semaphore("s_g"))

        def vinc(fn):
            n = E.inc(s_v, 1)
            def emit(v, fn=fn, n=n):
                v.wait_ge(s_v, n - 1)
                fn(v).then_inc(s_v, 1)
            E.p["vector"].append(emit)
            return n

        def sinc(fn):
            n = E.inc(s_s, 1)
            def emit(s, fn=fn, n=n):
                s.wait_ge(s_s, n - 1)
                fn(s).then_inc(s_s, 1)
            E.p["scalar"].append(emit)
            return n

        def tinc(fn):
            n = E.inc(s_t, 1)
            E.p["tensor"].append(lambda t: fn(t).then_inc(s_t, 1))
            return n

        def ginc(fn):
            n = E.inc(s_g, 1)
            def emit(g, fn=fn, n=n):
                g.wait_ge(s_g, n - 1)
                fn(g).then_inc(s_g, 1)
            E.p["gpsimd"].append(emit)
            return n

        def t_(fn):
            E.p["tensor"].append(fn)

        def dma(q, out, in_, sem=None, **kw):
            if sem is None:
                sem = s_ld if q == "sync" else s_go
            n = E.inc(sem, 16)
            E.p[q].append(lambda e: e.dma_start(out=out, in_=in_, **kw)
                          .then_inc(sem, 16))
            return n

        def wt(eng, sem, val):
            E.p[eng].append(lambda e: e.wait_ge(sem, val))

        def barrier(engs=("vector", "scalar", "tensor")):
            marks = {}
            for e in engs:
                sem = {"vector": s_v, "scalar": s_s, "tensor": s_t,
                       "gpsimd": s_g}[e]
                n = E.inc(sem, 1)
                def emit(x, sem=sem, n=n):
                    x.wait_ge(sem, n - 1)
                    x.nop().then_inc(sem, 1)
                E.p[e].append(emit)
                marks[e] = (sem, n)
            for e in engs:
                for o in engs:
                    if o != e:
                        wt(e, *marks[o])

        qd_ctr = [0]
        last_gf = [0]

        def qd_emit(X, OUT, F, gB, par=None, nb0=0):
            nb = F // 16
            a_s = qpool[:, nb0 * 16:nb0 * 16 + F]
            gA = qpool[:, I + nb0 * 16:I + nb0 * 16 + F]
            if par is None:
                par = qd_ctr[0] % 2
                qd_ctr[0] += 1
            bmx = qstat[:, par * 384 + nb0:par * 384 + nb0 + nb]
            rsc = qstat[:, par * 384 + 192 + nb0:par * 384 + 192 + nb0 + nb]
            vinc(lambda v: v.tensor_reduce(
                out=bmx, in_=X.rearrange("p (b k) -> p b k", k=16),
                axis=AXX, op=ALU.max, apply_absolute_value=True))
            vinc(lambda v: v.tensor_scalar(out=bmx, in0=bmx, scalar1=1.0 / 6.0,
                                           scalar2=1e-12, op0=ALU.mult, op1=ALU.max))
            vinc(lambda v: v.reciprocal(out=rsc, in_=bmx))
            vinc(lambda v: v.tensor_tensor(
                out=a_s.rearrange("p (b k) -> p b k", k=16),
                in0=X.rearrange("p (b k) -> p b k", k=16),
                in1=rsc.unsqueeze(2).broadcast_to((128, nb, 16)), op=ALU.mult))
            vinc(lambda v: v.tensor_tensor(out=a_s, in0=a_s, in1=a_s,
                                           op=ALU.mult))
            if nb0 == 0:
                wt("vector", s_g, last_gf[0])
            vinc(lambda v: v.tensor_scalar(out=gA, in0=a_s, scalar1=M2[0],
                                           scalar2=0.0, op0=ALU.is_gt, op1=ALU.add))
            for m2 in M2[1:4]:
                vinc(lambda v, m2=m2: v.scalar_tensor_tensor(
                    out=gA, in0=a_s, scalar=m2, in1=gA, op0=ALU.is_gt, op1=ALU.add))
            vinc(lambda v: v.tensor_scalar(out=gB, in0=a_s, scalar1=M2[6],
                                           scalar2=2.0, op0=ALU.is_gt, op1=ALU.mult))
            vinc(lambda v: v.scalar_tensor_tensor(
                out=gB, in0=a_s, scalar=M2[5], in1=gB, op0=ALU.is_gt, op1=ALU.add))
            vinc(lambda v: v.scalar_tensor_tensor(
                out=gB, in0=a_s, scalar=M2[4], in1=gB, op0=ALU.is_gt, op1=ALU.add))
            cmb = vinc(lambda v: v.scalar_tensor_tensor(
                out=gA, in0=gA, scalar=0.5, in1=gB, op0=ALU.mult, op1=ALU.add))
            wt("scalar", s_v, cmb)
            t2 = sinc(lambda s: s.activation(out=gB, in_=X, func=AF.Sign))
            wt("gpsimd", s_s, t2)
            ginc(lambda g: g.tensor_tensor(
                out=gB.rearrange("p (b k) -> p b k", k=16),
                in0=gB.rearrange("p (b k) -> p b k", k=16),
                in1=bmx.unsqueeze(2).broadcast_to((128, nb, 16)), op=ALU.mult))
            gf = ginc(lambda g: g.tensor_tensor(out=OUT, in0=gA, in1=gB,
                                                op=ALU.mult))
            last_gf[0] = gf
            return {"v": cmb, "s": t2, "g": gf}

        def ln_emit(SRC, DST, SQ, ntt):
            wt("scalar", s_v, E.val(s_v))
            mu = stats[:, 0:ntt]
            q2 = stats[:, 8:8 + ntt]
            var = stats[:, 16:16 + ntt]
            rstd = stats[:, 24:24 + ntt]
            vinc(lambda v: v.tensor_reduce(
                out=mu, in_=SRC.rearrange("p (tt d) -> p tt d", d=D),
                axis=AXX, op=ALU.add))
            t1 = sinc(lambda s: s.activation(out=SQ, in_=SRC, func=AF.Square))
            wt("vector", s_s, t1)
            vinc(lambda v: v.tensor_reduce(
                out=q2, in_=SQ.rearrange("p (tt d) -> p tt d", d=D),
                axis=AXX, op=ALU.add))
            vinc(lambda v: v.tensor_scalar(out=mu, in0=mu, scalar1=1.0 / D,
                                           scalar2=0.0, op0=ALU.mult, op1=ALU.add))
            vinc(lambda v: v.tensor_scalar(out=q2, in0=q2, scalar1=1.0 / D,
                                           scalar2=0.0, op0=ALU.mult, op1=ALU.add))
            vinc(lambda v: v.tensor_tensor(out=var, in0=mu, in1=mu, op=ALU.mult))
            t2 = vinc(lambda v: v.tensor_tensor(out=var, in0=q2, in1=var,
                                                op=ALU.subtract))
            wt("scalar", s_v, t2)
            sinc(lambda s: s.activation(out=rstd, in_=var, func=AF.Ln,
                                        bias=consts[:, 1:2]))
            t3 = sinc(lambda s: s.activation(out=rstd, in_=rstd, func=AF.Exp,
                                             scale=-0.5))
            wt("vector", s_s, t3)
            for tt in range(ntt):
                vinc(lambda v, tt=tt: v.tensor_scalar(
                    out=DST[:, tt * D:(tt + 1) * D],
                    in0=SRC[:, tt * D:(tt + 1) * D],
                    scalar1=mu[:, tt:tt + 1], scalar2=rstd[:, tt:tt + 1],
                    op0=ALU.subtract, op1=ALU.mult))

        ptr_mark = {}
        ptr_ctr = [0]

        def pe_tr(src, dst):
            slot = ptr_ctr[0] % 2
            ptr_ctr[0] += 1
            if slot in ptr_mark:
                wt("tensor", s_s, ptr_mark[slot])
            tn = tinc(lambda t, src=src, slot=slot: t.transpose(
                out=ptr[slot], in_=src, identity=ident))
            wt("scalar", s_t, tn)
            en = sinc(lambda s, dst=dst, slot=slot: s.activation(
                out=dst, in_=ptr[slot], func=AF.Copy))
            ptr_mark[slot] = en
            return en

        n0 = 0
        for dst, src in [(ident, ident_in), (consts, consts_in),
                         (acols, acols_in), (binc, binc_in), (bdtc, bdtc_in),
                         (dcol, dcol_in), (boutr, boutr_in), (gamr, gamr_in),
                         (betr, betr_in), (bbc_f[0:32, 0:1], bbc_in)]:
            n0 = dma("sync", dst, src, sem=s_w1)
        wt("scalar", s_w1, n0)
        wt("vector", s_w1, n0)
        t1 = sinc(lambda s: s.activation(out=acols, in_=acols, func=AF.Exp))
        wt("vector", s_s, t1)
        vinc(lambda v: v.tensor_scalar(out=acols, in0=acols, scalar1=-1.0,
                                       scalar2=0.0, op0=ALU.mult, op1=ALU.add))
        vinc(lambda v: v.tensor_scalar(out=binh, in0=binc, scalar1=0.5,
                                       scalar2=0.0, op0=ALU.mult, op1=ALU.add))

        def cast_weight(src_hbm, dst_scr, ncols):
            for gg in range(GT):
                sl = gg % 2
                nl = dma("sync", wstage[sl][:, 0:ncols],
                         src_hbm[gg * 128:(gg + 1) * 128, :], sem=s_wrow)
                wt("vector", s_wrow, nl)
                cn = vinc(lambda v, sl=sl, ncols=ncols: v.tensor_copy(
                    out=wqstage[sl][:, 0:ncols], in_=wstage[sl][:, 0:ncols]))
                wt("gpsimd", s_v, cn)
                gn = dma("gpsimd", dst_scr[gg * 128:(gg + 1) * 128, :],
                         wqstage[sl][:, 0:ncols])
                wt("sync", s_v, cn)
                wt("vector", s_go, gn)

        cast_weight(w_inT, s_winT, 2 * D)
        cast_weight(w_dtT, s_wdtT, D)
        cast_weight(w_outT, s_woutT, D)
        cast_weight(w_bcT, s_wbcT, 32)

        def prep_ffn(jobs):
            tiles = []
            for (src_hbm, dst_scr, nrows, ncols) in jobs:
                for rt in range(nrows // 128):
                    tiles.append((src_hbm, dst_scr, rt, ncols))
            loads = {}
            qds = {}
            trT = [0, 0]
            loads[0] = dma("sync", wstage[0][:, 0:tiles[0][3]],
                           tiles[0][0][tiles[0][2] * 128:
                                       (tiles[0][2] + 1) * 128, :], sem=s_wr2[0])
            for i, (src_hbm, dst_scr, rt, ncols) in enumerate(tiles):
                sl = i % 2
                if i + 1 < len(tiles):
                    s2_, d2_, rt2, nc2 = tiles[i + 1]
                    if i - 1 in qds:
                        wt("sync", s_v, qds[i - 1]["v"])
                        wt("sync", s_s, qds[i - 1]["s"])
                        wt("sync", s_g, qds[i - 1]["g"])
                    loads[i + 1] = dma("sync", wstage[(i + 1) % 2][:, 0:nc2],
                                       s2_[rt2 * 128:(rt2 + 1) * 128, :],
                                       sem=s_wr2[(i + 1) % 2])
                wt("vector", s_wr2[sl], loads[i])
                wt("scalar", s_wr2[sl], loads[i])
                wt("gpsimd", s_t, trT[sl])
                mkq = qd_emit(wstage[sl][:, 0:ncols], wqstage[sl][:, 0:ncols],
                              ncols, uprow[:, 0:ncols])
                qds[i] = mkq
                wt("tensor", s_g, mkq["g"])
                for cb in range(ncols // 128):
                    pe_tr(wqstage[sl][:, cb * 128:(cb + 1) * 128],
                          wq_ev[:, cb * 128:(cb + 1) * 128])
                trT[sl] = E.val(s_t)
                wt("gpsimd", s_s, E.val(s_s))
                gn = 0
                for cb in range(ncols // 128):
                    gn = dma("gpsimd", dst_scr[cb * 128:(cb + 1) * 128,
                                               rt * 128:(rt + 1) * 128],
                             wq_ev[:, cb * 128:(cb + 1) * 128])
                wt("scalar", s_go, gn)

        prep_ffn([(w_gate, s_wgT, I, D), (w_up, s_wuT, I, D),
                  (w_down, s_wdnT, D, I)])

        barrier()
        wt("sync", s_go, E.val(s_go))
        wt("gpsimd", s_v, E.val(s_v))

        nl = dma("sync", wout_r.rearrange("p (g d) -> p g d", g=GT),
                 s_woutT.rearrange("(g p) d -> p g d", p=128), sem=s_w1)
        nl = dma("sync", wbc_r.rearrange("p (g d) -> p g d", g=GT),
                 s_wbcT.rearrange("(g p) d -> p g d", p=128), sem=s_w1)
        wt("tensor", s_w1, nl)

        SKIP_P1 = bool(os.environ.get("K_SKIP_P1"))
        SKIP_P2 = bool(os.environ.get("K_SKIP_P2"))
        P1S = int(os.environ.get("K_P1STAGES", "99"))
        x_ch = x_hbm.rearrange("(c tt p) d -> c p tt d", p=128, tt=TT)
        xn_ch = s_xn.rearrange("(c tt p) d -> c p tt d", p=128, tt=TT)
        sg_ch = s_sg.rearrange("(c tt p) d -> c p tt d", p=128, tt=TT)
        ttd = lambda ap: ap.rearrange("p (tt d) -> p tt d", d=D)
        win_pan = s_winT.rearrange("(g p) o -> p g o", p=128)
        wdt_pan = s_wdtT.rearrange("(g p) o -> p g o", p=128)
        scan_marks = {}
        scan_i = [0]
        sc_hc = [0]
        uh_free = {}
        hb_free = {}
        fin_mark = {}
        ug_last = [0]
        xn_sp = [0]

        chunk_gate = [E.val(s_v)]
        for c in range(NCH if not SKIP_P1 else 0):
            par = c % 2
            wt("sync", s_v, chunk_gate[0])
            nx = dma("sync", ttd(xbuf), x_ch[c], sem=s_x)
            pan_cnt = {0: dma("sync", winr[:, 0:GT * 128],
                              win_pan[:, :, 0:128], sem=s_p2[0])}
            wt("vector", s_x, nx)
            wt("scalar", s_x, nx)
            wt("vector", s_go, E.val(s_go))

            ln_emit(xbuf, xn, xss, TT)
            xnv = vinc(lambda v: v.tensor_copy(out=xnb, in_=xn))
            wt("gpsimd", s_v, xnv)
            xn_sp[0] = dma("gpsimd", xn_ch[c], ttd(xn))
            wt("tensor", s_v, xnv)
            for tt in range(TT):
                for gg in range(GT):
                    pe_tr(xnb[:, tt * D + gg * 128: tt * D + (gg + 1) * 128],
                          xnT[:, gg * T + tt * 128: gg * T + (tt + 1) * 128])
            xnT_done = E.val(s_s)

            XZ_NJ = int(os.environ.get("K_XZ_NJ", "16"))
            XZ_MM = os.environ.get("K_XZ_MM", "1") == "1"
            XZ_EV = os.environ.get("K_XZ_EV", "1") == "1"
            last_mt = {}
            wt("tensor", s_s, xnT_done)
            for j in range((XZ_NJ if P1S >= 2 else 0)):
                pm = p_mm[j % 2]
                if j + 1 < 16:
                    if j - 1 in last_mt:
                        wt("sync", s_t, last_mt[j - 1])
                    pan_cnt[j + 1] = dma(
                        "sync", winr[:, ((j + 1) % 2) * GT * 128:
                                     (((j + 1) % 2) + 1) * GT * 128],
                        win_pan[:, :, (j + 1) * 128:(j + 2) * 128],
                        sem=s_p2[(j + 1) % 2])
                wt("tensor", s_p2[j % 2], pan_cnt[j])
                for gg in range(GT if XZ_MM else 0):
                    def mm(t, j=j, gg=gg, pm=pm):
                        return t.matmul(
                            out=pm[:, 0:T],
                            lhsT=winr[:, (j % 2) * GT * 128 + gg * 128:
                                      (j % 2) * GT * 128 + (gg + 1) * 128],
                            rhs=xnT[:, gg * T:(gg + 1) * T],
                            start=(gg == 0), stop=(gg == GT - 1))
                    if gg == GT - 1:
                        mt = tinc(mm)
                    else:
                        t_(lambda t, f=mm: f(t))
                if not XZ_MM:
                    mt = E.inc(s_t, 1)
                    E.p["tensor"].append(lambda t: t.nop().then_inc(s_t, 1))
                last_mt[j] = mt
                wt("vector", s_t, mt)
                if not XZ_EV:
                    ev = vinc(lambda v: v.nop())
                    wt("tensor", s_v, ev)
                    continue
                if j < 8:
                    ev = vinc(lambda v, j=j, pm=pm: v.tensor_scalar(
                        out=xss[:, j * T:(j + 1) * T], in0=pm[:, 0:T],
                        scalar1=binc[:, j:j + 1], scalar2=0.0,
                        op0=ALU.add, op1=ALU.add))
                else:
                    jj = j - 8
                    ZL = int(os.environ.get("K_ZL", "3"))
                    ZNS = bool(os.environ.get("K_Z_NOSCAL"))
                    ev0 = vinc(lambda v, j=j, jj=jj, pm=pm: v.tensor_scalar(
                        out=zs[:, jj * T:(jj + 1) * T], in0=pm[:, 0:T],
                        scalar1=binc[:, j:j + 1], scalar2=0.0,
                        op0=ALU.add, op1=ALU.add))
                    ev = ev0
                    if ZL >= 2:
                        if ZNS:
                            ev = vinc(lambda v, jj=jj: v.tensor_copy(
                                out=ln_scr[:, 0:T],
                                in_=zs[:, jj * T:(jj + 1) * T]))
                        else:
                            wt("scalar", s_v, ev0)
                            th = sinc(lambda s, jj=jj: s.activation(
                                out=ln_scr[:, 0:T],
                                in_=zs[:, jj * T:(jj + 1) * T],
                                func=AF.Sigmoid))
                            wt("vector", s_s, th)
                    if ZL >= 3:
                        ev = vinc(lambda v, jj=jj: v.tensor_tensor(
                            out=zs[:, jj * T:(jj + 1) * T],
                            in0=zs[:, jj * T:(jj + 1) * T],
                            in1=ln_scr[:, 0:T], op=ALU.mult))
                    wt("scalar", s_v, ev)
                wt("tensor", s_v, ev)

            xssb = xnT
            if P1S >= 3:
                wt("vector", s_t, last_mt[15])
                csb = vinc(lambda v: v.tensor_copy(out=xssb, in_=xss))
                wt("tensor", s_v, csb)

            if P1S >= 3:
                wt("sync", s_t, last_mt[14])
                dt_cnt = {0: dma("sync", wdtr[:, 0:GT * 128],
                                 wdt_pan[:, :, 0:128], sem=s_p2[0])}
            for j in range(8 if P1S >= 3 else 0):
                pm = p_mm[j % 2]
                if j + 1 < 8:
                    if j - 1 in last_mt:
                        wt("sync", s_t, last_mt[j - 1])
                    else:
                        wt("sync", s_t, last_mt[15])
                    dt_cnt[j + 1] = dma(
                        "sync", wdtr[:, ((j + 1) % 2) * GT * 128:
                                     (((j + 1) % 2) + 1) * GT * 128],
                        wdt_pan[:, :, (j + 1) * 128:(j + 2) * 128],
                        sem=s_p2[(j + 1) % 2])
                wt("tensor", s_p2[j % 2], dt_cnt[j])
                for gg in range(GT):
                    def mm(t, j=j, gg=gg, pm=pm):
                        return t.matmul(
                            out=pm[:, 0:T],
                            lhsT=wdtr[:, (j % 2) * GT * 128 + gg * 128:
                                      (j % 2) * GT * 128 + (gg + 1) * 128],
                            rhs=xssb[:, gg * T:(gg + 1) * T],
                            start=(gg == 0), stop=(gg == GT - 1))
                    if gg == GT - 1:
                        mt = tinc(mm)
                    else:
                        t_(lambda t, f=mm: f(t))
                last_mt[j] = mt
                wt("vector", s_t, mt)
                xb = vinc(lambda v, j=j, pm=pm: v.tensor_scalar(
                    out=ln_scr[:, T:2 * T], in0=pm[:, 0:T],
                    scalar1=bdtc[:, j:j + 1], scalar2=0.0,
                    op0=ALU.add, op1=ALU.add))
                wt("scalar", s_v, xb)
                sinc(lambda s: s.activation(out=ln_scr[:, 0:T],
                                            in_=ln_scr[:, T:2 * T],
                                            func=AF.Abs))
                sinc(lambda s: s.activation(out=ln_scr[:, 0:T],
                                            in_=ln_scr[:, 0:T],
                                            func=AF.Exp, scale=-1.0))
                sinc(lambda s: s.activation(out=ln_scr[:, 0:T],
                                            in_=ln_scr[:, 0:T],
                                            func=AF.Ln, bias=consts[:, 0:1]))
                rl = sinc(lambda s: s.activation(
                    out=ln_scr[:, T:2 * T], in_=ln_scr[:, T:2 * T],
                    func=AF.Relu))
                wt("vector", s_s, rl)
                ev = vinc(lambda v, j=j: v.tensor_tensor(
                    out=dts[:, j * T:(j + 1) * T], in0=ln_scr[:, 0:T],
                    in1=ln_scr[:, T:2 * T], op=ALU.add))
                wt("tensor", s_v, ev)
                wt("scalar", s_v, ev)

            for gg in range(GT if P1S >= 4 else 0):
                def mm(t, gg=gg):
                    return t.matmul(
                        out=PSF[0:32, 0:T],
                        lhsT=wbc_r[:, gg * 32:(gg + 1) * 32],
                        rhs=xssb[:, gg * T:(gg + 1) * T],
                        start=(gg == 0), stop=(gg == GT - 1))
                if gg == GT - 1:
                    mt = tinc(mm)
                else:
                    t_(lambda t, f=mm: f(t))
            if P1S >= 4:
                wt("vector", s_t, mt)
                bcv = vinc(lambda v: v.tensor_scalar(
                    out=bcbf[0:32, :], in0=PSF[0:32, 0:T],
                    scalar1=bbc_f[0:32, 0:1], scalar2=0.0,
                    op0=ALU.add, op1=ALU.add))
                wt("tensor", s_v, bcv)
                wt("gpsimd", s_v, bcv)
                b1 = dma("gpsimd", s_bc[par], bcbf[0:32, :])
                wt("sync", s_go, b1)
                dma("sync", brep,
                    s_bc[par, 0:16].rearrange("n t -> (n t)").partition_broadcast(128),
                    sem=s_x)
                b3 = dma("sync", crep,
                         s_bc[par, 16:32].rearrange("n t -> (n t)").partition_broadcast(128),
                         sem=s_x)
                wt("vector", s_x, b3)
                bc_mt = mt

            wt("vector", s_go, E.val(s_go))
            for gg in range(GT if P1S >= 5 else 0):
                vinc(lambda v, gg=gg: v.tensor_tensor(
                    out=dtxb, in0=dts[:, gg * T:(gg + 1) * T],
                    in1=xss[:, gg * T:(gg + 1) * T], op=ALU.mult))
                for half in range(2):
                    vinc(lambda v, half=half: v.tensor_tensor(
                        out=uhalf.rearrange("p (n t) -> p n t", n=8),
                        in0=dtxb.unsqueeze(1).broadcast_to((128, 8, T)),
                        in1=brep[:, half * 8 * T:(half * 8 + 8) * T]
                        .rearrange("p (n t) -> p n t", n=8),
                        op=ALU.mult))
                    for k in range(8):
                        n_ = half * 8 + k
                        i_ = scan_i[0]
                        slot = i_ % 4
                        if i_ >= 4:
                            wt("scalar", s_v, scan_marks[i_ - 4])
                        da = sinc(lambda s, gg=gg, n_=n_, slot=slot:
                                  s.activation(
                                      out=dA[:, slot * T:(slot + 1) * T],
                                      in_=dts[:, gg * T:(gg + 1) * T],
                                      func=AF.Exp,
                                      scale=acols[:, gg * 16 + n_:
                                                  gg * 16 + n_ + 1]))
                        wt("vector", s_s, da)
                        if c == 0:
                            init = 0.0
                        else:
                            init = carry[:, (1 - par) * 128 + gg * 16 + n_:
                                         (1 - par) * 128 + gg * 16 + n_ + 1]
                        mk = vinc(lambda v, k=k, slot=slot, init=init:
                                  v.tensor_tensor_scan(
                                      out=hbuf[:, k * T:(k + 1) * T],
                                      data0=dA[:, slot * T:(slot + 1) * T],
                                      data1=uhalf[:, k * T:(k + 1) * T],
                                      initial=init, op0=ALU.mult, op1=ALU.add))
                        scan_marks[i_] = mk
                        scan_i[0] += 1
                    vinc(lambda v, gg=gg, half=half, par=par: v.tensor_copy(
                        out=carry[:, par * 128 + gg * 16 + half * 8:
                                  par * 128 + gg * 16 + half * 8 + 8],
                        in_=hbuf[:, T - 1::T]))
                    vinc(lambda v, half=half: v.tensor_tensor(
                        out=hbuf, in0=hbuf,
                        in1=crep[:, half * 8 * T:(half * 8 + 8) * T],
                        op=ALU.mult))
                    w = 8
                    while w > 1:
                        w //= 2
                        vinc(lambda v, w=w: v.tensor_tensor(
                            out=hbuf[:, 0:w * T], in0=hbuf[:, 0:w * T],
                            in1=hbuf[:, w * T:2 * w * T], op=ALU.add))
                    vinc(lambda v, half=half: v.tensor_copy(
                        out=ln_scr[:, half * T:(half + 1) * T],
                        in_=hbuf[:, 0:T]))
                vinc(lambda v: v.tensor_tensor(
                    out=ln_scr[:, 0:T], in0=ln_scr[:, 0:T],
                    in1=ln_scr[:, T:2 * T], op=ALU.add))
                vinc(lambda v, gg=gg: v.scalar_tensor_tensor(
                    out=ln_scr[:, 0:T], in0=xss[:, gg * T:(gg + 1) * T],
                    scalar=dcol[:, gg:gg + 1], in1=ln_scr[:, 0:T],
                    op0=ALU.mult, op1=ALU.add))
                vinc(lambda v, gg=gg: v.tensor_tensor(
                    out=y2b[:, gg * T:(gg + 1) * T], in0=ln_scr[:, 0:T],
                    in1=zs[:, gg * T:(gg + 1) * T], op=ALU.mult))
            barrier()

            if P1S >= 6:
                wt("vector", s_go, xn_sp[0])
                vinc(lambda v: v.tensor_tensor(
                    out=ttd(xn), in0=ttd(xn),
                    in1=boutr.unsqueeze(1).broadcast_to((128, TT, D)),
                    op=ALU.add))
            for tt in range(TT if P1S >= 6 else 0):
                for dh in range(2):
                    pm = p_mm[dh]
                    for gg in range(GT):
                        def mm(t, tt=tt, gg=gg, dh=dh, pm=pm):
                            return t.matmul(
                                out=pm[:, 0:512],
                                lhsT=y2b[:, gg * T + tt * 128:
                                         gg * T + (tt + 1) * 128],
                                rhs=wout_r[:, gg * D + dh * 512:
                                           gg * D + (dh + 1) * 512],
                                start=(gg == 0), stop=(gg == GT - 1))
                        if gg == GT - 1:
                            mt = tinc(mm)
                        else:
                            t_(lambda t, f=mm: f(t))
                    wt("vector", s_t, mt)
                    ev = vinc(lambda v, tt=tt, dh=dh, pm=pm:
                              v.scalar_tensor_tensor(
                        out=pre[:, tt * D + dh * 512: tt * D + (dh + 1) * 512],
                        in0=pm[:, 0:512], scalar=0.0,
                        in1=xn[:, tt * D + dh * 512: tt * D + (dh + 1) * 512],
                        op0=ALU.add, op1=ALU.add))
                    wt("tensor", s_v, ev)

            if P1S >= 7:
                ln_emit(pre, pre, xss, TT)
            for tt in range(TT if P1S >= 7 else 0):
                vinc(lambda v, tt=tt: v.tensor_tensor(
                    out=pre[:, tt * D:(tt + 1) * D],
                    in0=pre[:, tt * D:(tt + 1) * D], in1=gamr, op=ALU.mult))
                vinc(lambda v, tt=tt: v.tensor_tensor(
                    out=pre[:, tt * D:(tt + 1) * D],
                    in0=pre[:, tt * D:(tt + 1) * D], in1=betr, op=ALU.add))
            if P1S >= 7:
                sv = vinc(lambda v: v.tensor_copy(out=xnb, in_=pre))
                wt("gpsimd", s_v, sv)
                dma("gpsimd", sg_ch[c], ttd(xnb))
            barrier()
            chunk_gate[0] = E.val(s_v)

        wt("sync", s_go, E.val(s_go))

        wt("sync", s_v, chunk_gate[0])
        ng = dma("sync", wg_r.rearrange("p (g i) -> p g i", g=GT),
                 s_wgT.rearrange("(g p) i -> p g i", p=128), sem=s_w1)
        wt("tensor", s_w1, ng)
        wu_pan = s_wuT.rearrange("(g p) i -> p g i", p=128)
        x_tl = x_hbm.rearrange("(q p) d -> q p d", p=128)
        xn_tl = s_xn.rearrange("(q p) d -> q p d", p=128)
        sg_tl = s_sg.rearrange("(q p) d -> q p d", p=128)
        out_tl = out_hbm.rearrange("(q p) d -> q p d", p=128)
        up_mark = {}
        up_ld = {}
        dn_mark = {}
        pgu_free = {}
        pgu_ctr = [0]
        dn_evac_prev = [0]
        tr_last = [0]
        tl_ld = {}
        tl_free = {}

        def up_load(gsec):
            sec = gsec % 6
            return dma("scalar",
                       wur[:, (gsec % 2) * GT * 512:
                           (gsec % 2) * GT * 512 + GT * SECW[sec]],
                       wu_pan[:, :, SECO[sec]:SECO[sec] + SECW[sec]],
                       sem=s_p2[gsec % 2])

        def p2_loads(qq, par2):
            sm = s_tl2[par2]
            dma("gpsimd", xn2[:, par2 * 2 * D:par2 * 2 * D + D],
                xn_tl[2 * qq], sem=sm)
            dma("gpsimd", xn2[:, par2 * 2 * D + D:(par2 + 1) * 2 * D],
                xn_tl[2 * qq + 1], sem=sm)
            dma("gpsimd", s2[:, par2 * 2 * D:par2 * 2 * D + D],
                sg_tl[2 * qq], sem=sm)
            return dma("gpsimd", s2[:, par2 * 2 * D + D:(par2 + 1) * 2 * D],
                       sg_tl[2 * qq + 1], sem=sm)

        if NPAIR and not SKIP_P2:
            tl_ld[0] = p2_loads(0, 0)
        pair_mk = {}
        xtr_last = [0]
        dn_v_last = [0]

        def pre_emit(qq):
            par2 = qq % 2
            wt("vector", s_tl2[par2], tl_ld[qq])
            wt("scalar", s_tl2[par2], tl_ld[qq])
            wt("gpsimd", s_t, xtr_last[0])
            pair_mk[qq] = qd_emit(xn2[:, par2 * 2 * D:(par2 + 1) * 2 * D],
                                  xnq, 2 * D, uprow[:, 0:2 * D])

        if NPAIR and not SKIP_P2:
            pre_emit(0)
        for q in range(NPAIR if not SKIP_P2 else 0):
            q0, q1 = 2 * q, 2 * q + 1
            par2 = q % 2
            s2v = s2[:, par2 * 2 * D:(par2 + 1) * 2 * D]
            mk = pair_mk[q]
            if q + 1 < NPAIR:
                if q - 1 in tl_free:
                    wt("gpsimd", s_v, tl_free[q - 1][0])
                    wt("gpsimd", s_s, tl_free[q - 1][1])
                tl_ld[q + 1] = p2_loads(q + 1, (q + 1) % 2)
            wt("sync", s_v, tl_free.get(q - 1, (0, 0))[0])
            nx2 = dma("sync", x2[:, 0:D], x_tl[q0], sem=s_x)
            nx2 = dma("sync", x2[:, D:2 * D], x_tl[q1], sem=s_x)
            for pre_s in (0, 1):
                gsec = q * 6 + pre_s
                if gsec - 2 in up_mark:
                    wt("scalar", s_t, up_mark[gsec - 2])
                up_ld[gsec] = up_load(gsec)
            wt("scalar", s_go, E.val(s_go))

            wt("tensor", s_g, mk["g"])
            for ti in range(2):
                for gg in range(GT):
                    pe_tr(xnq[:, ti * D + gg * 128: ti * D + (gg + 1) * 128],
                          xnqT[:, gg * 256 + ti * 128: gg * 256 + (ti + 1) * 128])
            xtr_last[0] = E.val(s_t)
            xnqT_done = E.val(s_s)

            for sec in range(6):
                w_ = SECW[sec]
                gsec = q * 6 + sec
                uslot = gsec % 2
                wt("tensor", s_p2[uslot], up_ld[gsec])
                if sec == 0:
                    wt("tensor", s_s, xnqT_done)
                    wt("scalar", s_v, dn_v_last[0])
                for ti in range(2):
                    for which in range(2):
                        pslot = pgu_ctr[0] % 2
                        ps = pgu[pslot]
                        pgu_ctr[0] += 1
                        if pslot in pgu_free:
                            wt("tensor", s_s, pgu_free[pslot])
                        for gg in range(GT):
                            if which == 0:
                                def mmf(t, ti=ti, sec=sec, gg=gg, w_=w_, ps=ps):
                                    return t.matmul(
                                        out=ps[:, 0:w_],
                                        lhsT=xnqT[:, gg * 256 + ti * 128:
                                                  gg * 256 + (ti + 1) * 128],
                                        rhs=wg_r[:, gg * I + SECO[sec]:
                                                 gg * I + SECO[sec] + w_],
                                        start=(gg == 0), stop=(gg == GT - 1))
                            else:
                                def mmf(t, ti=ti, gg=gg, w_=w_, uslot=uslot,
                                        ps=ps):
                                    return t.matmul(
                                        out=ps[:, 0:w_],
                                        lhsT=xnqT[:, gg * 256 + ti * 128:
                                                  gg * 256 + (ti + 1) * 128],
                                        rhs=wur[:, uslot * GT * 512 + gg * w_:
                                                uslot * GT * 512 + (gg + 1) * w_],
                                        start=(gg == 0), stop=(gg == GT - 1))
                            if gg == GT - 1:
                                mt = tinc(mmf)
                            else:
                                t_(lambda t, f=mmf: f(t))
                        wt("scalar", s_t, mt)
                        dstb = hstage if which == 0 else uprow
                        pgu_free[pslot] = sinc(
                            lambda s, ps=ps, dstb=dstb, ti=ti, sec=sec, w_=w_:
                            s.activation(out=dstb[:, ti * I + SECO[sec]:
                                                  ti * I + SECO[sec] + w_],
                                         in_=ps[:, 0:w_], func=AF.Copy))
                        if which == 1:
                            up_mark[gsec] = mt
                if sec + 2 < 6:
                    wt("scalar", s_t, up_mark[gsec])
                    up_ld[gsec + 2] = up_load(gsec + 2)

            SLC = [(0, 768), (768, 768), (1536, 768), (2304, 512)]
            dn_ready = {}
            prev_g = mk["g"]
            qd_h1 = None
            wt("gpsimd", s_t, tr_last[0])
            for ti in range(2):
                hsv = hstage[:, ti * I:(ti + 1) * I]
                upv = uprow[:, ti * I:(ti + 1) * I]
                hqv = hq[:, ti * I:(ti + 1) * I]
                parh = qd_ctr[0] % 2
                qd_ctr[0] += 1
                wt("scalar", s_g, prev_g)
                sigm = {}
                for (c0, cw) in SLC:
                    sigm[c0] = sinc(lambda s, hsv=hsv, c0=c0, cw=cw:
                                    s.activation(
                        out=qpool[:, c0:c0 + cw], in_=hsv[:, c0:c0 + cw],
                        func=AF.Sigmoid))
                for (c0, cw) in SLC:
                    wt("vector", s_s, sigm[c0])
                    vinc(lambda v, hsv=hsv, c0=c0, cw=cw: v.tensor_tensor(
                        out=hsv[:, c0:c0 + cw], in0=hsv[:, c0:c0 + cw],
                        in1=qpool[:, c0:c0 + cw], op=ALU.mult))
                    vinc(lambda v, hsv=hsv, upv=upv, c0=c0, cw=cw:
                         v.tensor_tensor(
                             out=hsv[:, c0:c0 + cw], in0=hsv[:, c0:c0 + cw],
                             in1=upv[:, c0:c0 + cw], op=ALU.mult))
                    mkh = qd_emit(hsv[:, c0:c0 + cw], hqv[:, c0:c0 + cw],
                                  cw, upv[:, c0:c0 + cw],
                                  par=parh, nb0=c0 // 16)
                    wt("tensor", s_g, mkh["g"])
                    for ib in range(c0 // 128, (c0 + cw) // 128):
                        en = pe_tr(hqv[:, ib * 128:(ib + 1) * 128],
                                   hqT[:, ib * 256 + ti * 128:
                                       ib * 256 + (ti + 1) * 128])
                        if ti == 1:
                            dn_ready[ib] = en
                prev_g = mkh["g"]
                qd_h1 = mkh
            tr_last[0] = E.val(s_t)

            wt("tensor", s_s, dn_evac_prev[0])
            for ib in range(NIB):
                slot = ib % 2
                gib = q * NIB + ib
                if gib - 2 in dn_mark:
                    wt("sync", s_t, dn_mark[gib - 2])
                nd = dma("sync", dnr[:, slot * D:(slot + 1) * D],
                         s_wdnT[ib * 128:(ib + 1) * 128, :], sem=s_d4[slot])
                wt("tensor", s_d4[slot], nd)
                wt("tensor", s_s, dn_ready[ib])
                for ti in range(2):
                    for dh in range(2):
                        def mmd(t, ib=ib, ti=ti, dh=dh, slot=slot):
                            return t.matmul(
                                out=p_dn[ti * 2 + dh][:, 0:512],
                                lhsT=hqT[:, ib * 256 + ti * 128:
                                         ib * 256 + (ti + 1) * 128],
                                rhs=dnr[:, slot * D + dh * 512:
                                        slot * D + (dh + 1) * 512],
                                start=(ib == 0), stop=(ib == NIB - 1))
                        if ti == 1 and dh == 1:
                            mt = tinc(mmd)
                        else:
                            t_(lambda t, f=mmd: f(t))
                dn_mark[gib] = mt
            if q + 1 < NPAIR:
                pre_emit(q + 1)
            wt("scalar", s_t, mt)
            wt("scalar", s_g, qd_h1["g"])
            ev_marks = []
            it = 0
            for ti in range(2):
                wt("vector", s_go, E.val(s_go))
                for dh in range(2):
                    pm = p_dn[ti * 2 + dh]
                    p_ = it % 2
                    g0 = gsc[:, p_ * 1024:p_ * 1024 + 512]
                    g1 = gsc[:, p_ * 1024 + 512:(p_ + 1) * 1024]
                    if it >= 2:
                        wt("scalar", s_v, ev_marks[it - 2])
                    sinc(lambda s, pm=pm, g0=g0: s.activation(
                        out=g0, in_=pm[:, 0:512], func=AF.Copy))
                    th = sinc(lambda s, g0=g0, g1=g1: s.activation(
                        out=g1, in_=g0, func=AF.Sigmoid))
                    wt("vector", s_s, th)
                    vinc(lambda v, g0=g0, g1=g1: v.tensor_tensor(
                        out=g0, in0=g0, in1=g1, op=ALU.mult))
                    o0 = ti * D + dh * 512
                    wt("vector", s_x, nx2)
                    vinc(lambda v, o0=o0, dh=dh, s2v=s2v, g0=g0:
                         v.tensor_tensor(
                        out=obuf[:, dh * 512:(dh + 1) * 512], in0=g0,
                        in1=s2v[:, o0:o0 + 512], op=ALU.mult))
                    ev = vinc(lambda v, o0=o0, dh=dh: v.tensor_tensor(
                        out=obuf[:, dh * 512:(dh + 1) * 512],
                        in0=obuf[:, dh * 512:(dh + 1) * 512],
                        in1=x2[:, o0:o0 + 512], op=ALU.add))
                    ev_marks.append(ev)
                    it += 1
                ov = E.val(s_v)
                wt("gpsimd", s_v, ov)
                dma("gpsimd", out_tl[q0 + ti], obuf)
            dn_evac_prev[0] = E.val(s_s)
            dn_v_last[0] = ov
            tl_free[q] = (ov, mk["s"])

        E.p["gpsimd"].append(lambda g: g.wait_ge(s_go, E.val(s_go)))

        block = es.enter_context(nc.Block())

        @block.sync
        def _(sy):
            for fn in E.p["sync"]:
                fn(sy)

        @block.vector
        def _(v):
            for fn in E.p["vector"]:
                fn(v)

        @block.scalar
        def _(s):
            for fn in E.p["scalar"]:
                fn(s)

        @block.gpsimd
        def _(g):
            for fn in E.p["gpsimd"]:
                fn(g)

        @block.tensor
        def _(t):
            for fn in E.p["tensor"]:
                fn(t)

    return nc


def _host_layouts(inputs):
    f32 = np.float32
    g = lambda k: np.asarray(inputs[k], f32)
    w_in = g("L0_attn_w_in"); b_in = g("L0_attn_b_in")
    w_dt = g("L0_attn_w_dt"); b_dt = g("L0_attn_b_dt")
    A_log = g("L0_attn_A_log")
    w_B = g("L0_attn_w_B"); b_B = g("L0_attn_b_B")
    w_C = g("L0_attn_w_C"); b_C = g("L0_attn_b_C")
    Dv = g("L0_attn_D")
    w_out = g("L0_attn_w_out"); b_out = g("L0_attn_b_out")
    gamma = g("L0_attn_gamma"); beta = g("L0_attn_beta")
    w_gate = g("L0_ffn_w_gate"); w_up = g("L0_ffn_w_up")
    w_dn = g("L0_ffn_w_down")

    bf = ml_dtypes.bfloat16
    consts = np.zeros((128, 8), f32)
    consts[:, 0] = 1.0
    consts[:, 1] = LN_EPS
    return {
        "w_inT": np.ascontiguousarray(w_in.T),
        "w_dtT": np.ascontiguousarray(w_dt.T),
        "w_outT": np.ascontiguousarray(w_out.T),
        "w_bcT": np.ascontiguousarray(np.concatenate([w_B.T, w_C.T], axis=1)),
        "w_gate": np.ascontiguousarray(w_gate),
        "w_up": np.ascontiguousarray(w_up),
        "w_down": np.ascontiguousarray(w_dn),
        "a_cols_log": np.ascontiguousarray(
            A_log.reshape(8, 128, NST).transpose(1, 0, 2).reshape(128, 128)),
        "b_in_cols": np.ascontiguousarray(b_in.reshape(16, 128).T),
        "b_dt_cols": np.ascontiguousarray(b_dt.reshape(8, 128).T),
        "bbc_col": np.ascontiguousarray(
            np.concatenate([b_B, b_C]).reshape(32, 1)),
        "d_cols": np.ascontiguousarray(Dv.reshape(8, 128).T),
        "b_out_rep": np.ascontiguousarray(
            np.broadcast_to(b_out, (128, D))).astype(bf),
        "gamma_rep": np.ascontiguousarray(
            np.broadcast_to(gamma, (128, D))).astype(bf),
        "beta_rep": np.ascontiguousarray(
            np.broadcast_to(beta, (128, D))).astype(bf),
        "ident_bf": np.eye(128, dtype=bf),
        "consts": consts,
    }


def kernel(**inputs):
    x = np.asarray(inputs["L0_attn_x"], np.float32)
    B, L, _ = x.shape
    shared = _host_layouts(inputs)
    nc = build_program(L=L, T=512)
    in_maps = []
    for b in range(B):
        m = dict(shared)
        m["x_b"] = np.ascontiguousarray(x[b])
        in_maps.append(m)
    res = run_bass_kernel_spmd(nc, in_maps, core_ids=list(range(B)))
    return np.stack([np.asarray(res.results[b]["out_b"]) for b in range(B)],
                    axis=0)
